# version 09
# baseline (speedup 1.0000x reference)
import contextlib
import os
import numpy as np
import ml_dtypes
import concourse.bass as bass
import concourse.mybir as mybir
from concourse.bass_utils import run_bass_kernel_spmd

F32 = mybir.dt.float32
BF16 = mybir.dt.bfloat16
AF = mybir.ActivationFunctionType
ALU = mybir.AluOpType
AX = mybir.AxisListType

D, DC, T, NT, TB, NTB = 1024, 8, 2048, 16, 512, 4
S, NH, DKH, KT = 4096, 16, 64, 32
E0, H0 = 4, 2730
HC0 = [128] * 21 + [42]
SLAB0 = [(0, 6), (6, 12), (12, 18), (18, 22)]
E1, F1 = 14, 2048
SLAB1 = [(0, 8), (8, 16)]
NEG = -1.0e30
BF = ml_dtypes.bfloat16


class Sem:
    def __init__(self, h):
        self.h = h
        self.n = 0

    def bump(self, k=1):
        self.n += k
        return self.n

    def at(self):
        return (self, self.n)


class Em:
    ENGS = ("tensor", "scalar", "vector", "gpsimd", "sync")

    def __init__(self, nc, stack):
        self.nc = nc
        self.stack = stack
        self.sem_stack = stack
        self.sems = {}
        self.prefix = ""
        self.phase_waits = None
        self.phase_claimed = set()
        self.ops = {e: [] for e in self.ENGS}
        self._si = 0
        self.paranoid = os.environ.get("KSYNC", "") == "paranoid"
        self.psems = {}

    def sem(self, name=None):
        if name is not None and name in self.sems:
            return self.sems[name]
        h = self.sem_stack.enter_context(self.nc.semaphore(name or f"s{self._si}"))
        self._si += 1
        sm = Sem(h)
        if name is not None:
            self.sems[name] = sm
        return sm

    def phase_barrier(self, waits):
        self.phase_waits = list(waits)
        self.phase_claimed = set()

    def op(self, eng, fn, w=(), s=None, k=1):
        if self.phase_waits is not None and eng not in self.phase_claimed:
            w = list(self.phase_waits) + list(w)
            self.phase_claimed.add(eng)
        paranoid = self.paranoid and eng in ("tensor", "scalar", "vector")
        if paranoid:
            w = list(w) + [o.at() for e2, o in self.psems.items() if e2 != eng]
        if s is not None:
            s.bump(k)
        w = tuple((sem.h, val) for sem, val in w)
        if paranoid:
            ps = self.psems[eng]
            ps.bump(1)

            def body(E, fn=fn, w=w, s=s, k=k, ps=ps):
                for h, val in w:
                    E.wait_ge(h, val)
                inst = fn(E)
                if s is not None:
                    inst.then_inc(s.h, k)
                E.nop().then_inc(ps.h, 1)
            self.ops[eng].append(body)
            return

        def body(E, fn=fn, w=w, s=s, k=k):
            for h, val in w:
                E.wait_ge(h, val)
            inst = fn(E)
            if s is not None:
                inst.then_inc(s.h, k)
        self.ops[eng].append(body)

    def barrier_wait(self, eng, w):
        self.op(eng, lambda E: E.nop(), w=w)

    def flush(self, name=None):
        nc = self.nc
        todo = {e: lst for e, lst in self.ops.items() if lst}
        self.ops = {e: [] for e in self.ENGS}
        with nc.Block(name) as blk:
            for e, lst in todo.items():
                def run(E, lst=lst):
                    for f in lst:
                        f(E)
                getattr(blk, e)(run)


def sfence(em, P, src, s_=None, k=1, w=()):
    p, f = src.shape[-2], src.shape[-1]
    f = min(f, 256)
    em.op("scalar", lambda sc, src=src, p=p, f=f: sc.copy(P.junk[0:p, 256:256 + f], src[0:p, 0:f]),
          w=w, s=s_, k=k)


def vfence(em, P, src, s_=None, k=1):
    p, f = src.shape[-2], src.shape[-1]
    f = min(f, 256)
    em.op("vector", lambda v, src=src, p=p, f=f: v.tensor_copy(P.junk[0:p, 0:f], src[0:p, 0:f]), s=s_, k=k)


def sbuf(em, name, shape, dt):
    return em.stack.enter_context(em.nc.sbuf_tensor(em.prefix + name, shape, dt))


def psum(em, name, shape, dt=F32):
    return em.stack.enter_context(em.nc.psum_tensor(name, shape, dt))


class Ctx:
    pass


def make_ctx(em):
    P = Ctx()
    if em.paranoid:
        em.psems = {e: em.sem(f"par_{e}") for e in ("tensor", "scalar", "vector")}
    P.XTBIG = sbuf(em, "XTBIG", [128, DC * T], F32)
    P.XHBIG = sbuf(em, "XHBIG", [128, DC * T], BF16)
    P.XT = [P.XTBIG[:, c * T:(c + 1) * T] for c in range(DC)]
    P.XH = [P.XHBIG[:, c * T:(c + 1) * T] for c in range(DC)]
    P.PA = psum(em, "PA", [128, 1024])
    P.PB = psum(em, "PB", [128, 1024])
    P.PY = psum(em, "PY", [128, 1024])
    P.PM = psum(em, "PM", [128, 512])
    P.P8 = psum(em, "P8", [128, 512])
    P.junk = sbuf(em, "junk", [128, 512], F32)
    P.st = em.sem("st")
    P.ss = em.sem("ss")
    P.sv = em.sem("sv")
    P.sg = em.sem("sg")
    P.sgd = em.sem("sgd")
    P.syd = em.sem("syd")
    P.xh_ready = None
    return P


def emit_load_xt(em, P, xt_dram):
    sld = em.sem("xtld")
    for c in range(DC):
        em.op("sync", lambda y, c=c: y.dma_start(
            out=P.XT[c], in_=xt_dram[c * 128:(c + 1) * 128, :]), s=sld, k=16)
    P.xt_ready = sld.at()


def emit_store_xt(em, P, xt_dram):
    sst = em.sem("xtst")
    w = [(P.sv, P.sv.n), (P.ss, P.ss.n)]
    for c in range(DC):
        em.op("sync", lambda y, c=c: y.dma_start(
            out=xt_dram[c * 128:(c + 1) * 128, :], in_=P.XT[c]),
            w=w if c == 0 else (), s=sst, k=16)
    em.barrier_wait("sync", [sst.at()])


def emit_tin(em, P, x_tok, C):
    idn = C["idn"]
    ring = [sbuf(em, f"tin{i}", [128, D], F32) for i in range(2)]
    sld = [em.sem("tinl0"), em.sem("tinl1")]
    evac_done = [(P.sv, 0), (P.sv, 0)]
    slots = [P.PA[:, 0:128], P.PA[:, 512:640], P.PB[:, 0:128], P.PB[:, 512:640]]
    slot_evac = [None, None, None, None]
    for ti in range(NT):
        b = ti % 2
        em.op("sync", lambda y, ti=ti, b=b: y.dma_start(
            out=ring[b][:], in_=x_tok[ti * 128:(ti + 1) * 128, :]),
            w=[evac_done[b]] if ti >= 2 else (), s=sld[b], k=16)
        dmaw = sld[b].at()
        for c in range(DC):
            sl = slots[c % 4]
            w2 = [dmaw] if c == 0 else []
            if slot_evac[c % 4] is not None:
                w2.append(slot_evac[c % 4])
            em.op("tensor", lambda t, c=c, b=b, sl=sl: t.transpose(
                sl, ring[b][:, c * 128:(c + 1) * 128], idn[:]),
                w=w2, s=P.st)
            tw = P.st.at()
            em.op("vector", lambda v, c=c, ti=ti, sl=sl: v.tensor_copy(
                P.XT[c][:, ti * 128:(ti + 1) * 128], sl),
                w=[tw], s=P.sv)
            slot_evac[c % 4] = P.sv.at()
        evac_done[b] = P.sv.at()
    P.xt_ready = P.sv.at()


def emit_tout(em, P, x_out, C):
    idn = C["idn"]
    ring = [sbuf(em, f"tout{i}", [128, D], F32) for i in range(2)]
    sst = [em.sem("touts0"), em.sem("touts1")]
    store_done = [(sst[0], 0), (sst[1], 0)]
    w0 = [(P.sv, P.sv.n), (P.ss, P.ss.n)]
    if getattr(P, "xt_ready", None) is not None:
        w0.append(P.xt_ready)
    slots = [P.PA[:, 0:128], P.PA[:, 512:640], P.PB[:, 0:128], P.PB[:, 512:640]]
    slot_evac = [None, None, None, None]
    for ti in range(NT):
        b = ti % 2
        for c in range(DC):
            sl = slots[c % 4]
            w2 = list(w0) if ti == 0 and c == 0 else []
            if slot_evac[c % 4] is not None:
                w2.append(slot_evac[c % 4])
            if c == 0 and ti >= 2:
                w2.append(store_done[b])
            em.op("tensor", lambda t, c=c, ti=ti, sl=sl: t.transpose(
                sl, P.XT[c][:, ti * 128:(ti + 1) * 128], idn[:]),
                w=w2, s=P.st)
            tw = P.st.at()
            em.op("vector", lambda v, c=c, b=b, sl=sl: v.tensor_copy(
                ring[b][:, c * 128:(c + 1) * 128], sl),
                w=[tw], s=P.sv)
            slot_evac[c % 4] = P.sv.at()
        em.op("sync", lambda y, ti=ti, b=b: y.dma_start(
            out=x_out[ti * 128:(ti + 1) * 128, :], in_=ring[b][:]),
            w=[P.sv.at()], s=sst[b], k=16)
        store_done[b] = sst[b].at()
    em.barrier_wait("sync", [sst[0].at(), sst[1].at()])


def emit_ln(em, P, C, sb):
    ones_sum, ones_bc = C["ones_sum"], C["ones_bc"]
    xc, xq = sb["xc"], sb["xq"]
    rows = sb["rows"]
    rbf = sb["rbf"]
    AB, BB, t1 = sb["AB"], sb["BB"], sb["t1"]
    r0, r1, r2, r3 = rows[0:1, 0:512], rows[0:1, 512:1024], rows[0:1, 1024:1536], rows[0:1, 1536:2048]
    abf, bbf = rbf[0:1, 0:512], rbf[0:1, 512:1024]

    xtw = getattr(P, "xt_ready", None)
    em.op("vector", lambda v: v.memset(rbf[0:4, :], 0.0))
    mm_hist = []
    row_done = None
    bc_done = None
    for tb in range(NTB):
        for c in range(DC):
            h = c % 2
            gi = tb * DC + c
            w_cast = [mm_hist[gi - 2]] if gi >= 2 else []
            if xtw is not None and tb == 0 and c == 0:
                w_cast = [xtw] + w_cast
            if row_done is not None and c == 0:
                w_cast.append(row_done)
            em.op("scalar", lambda s, c=c, tb=tb, h=h: s.copy(
                xc[:, h * 512:(h + 1) * 512], P.XT[c][:, tb * TB:(tb + 1) * TB]), w=w_cast, s=P.ss)
            em.op("scalar", lambda s, c=c, tb=tb, h=h: s.activation(
                xq[:, h * 512:(h + 1) * 512], P.XT[c][:, tb * TB:(tb + 1) * TB], AF.Square), s=P.ss)
            cw = P.ss.at()
            wmm = [cw]
            if c == 0 and tb > 0:
                wmm.append(ev_done)
            em.op("tensor", lambda t, c=c, h=h: t.matmul(
                P.PA[0:1, 0:512], ones_sum[:, 0:1], xc[:, h * 512:(h + 1) * 512],
                start=(c == 0), stop=(c == DC - 1)), w=wmm, s=None)
            em.op("tensor", lambda t, c=c, h=h: t.matmul(
                P.PA[0:1, 512:1024], ones_sum[:, 0:1], xq[:, h * 512:(h + 1) * 512],
                start=(c == 0), stop=(c == DC - 1)), s=P.st)
            mm_hist.append(P.st.at())
        mmw = P.st.at()
        em.op("scalar", lambda s: s.copy(r0, P.PA[0:1, 0:512]), w=[mmw], s=P.ss)
        em.op("scalar", lambda s: s.copy(r1, P.PA[0:1, 512:1024]), s=P.ss)
        ev_done = P.ss.at()
        em.op("vector", lambda v: v.tensor_scalar(r0, r0, 1.0 / D, None, ALU.mult), w=[ev_done])
        em.op("vector", lambda v: v.tensor_scalar(r1, r1, 1.0 / D, None, ALU.mult))
        em.op("vector", lambda v: v.scalar_tensor_tensor(r2, r0, 1.0, r0, ALU.mult, ALU.mult))
        vfence(em, P, r2)
        em.op("vector", lambda v: v.tensor_tensor(r1, r1, r2, ALU.subtract), s=P.sv)
        em.op("scalar", lambda s: s.activation(r1, r1, AF.Sqrt, bias=C["eps"][0:1, 0:1], scale=1.0),
              w=[P.sv.at()], s=P.ss)
        em.op("vector", lambda v: v.reciprocal(r3, r1), w=[P.ss.at()])
        vfence(em, P, r3)
        em.op("vector", lambda v: v.scalar_tensor_tensor(r2, r0, -1.0, r3, ALU.mult, ALU.mult))
        em.op("vector", lambda v: v.tensor_copy(abf, r3))
        em.op("vector", lambda v: v.tensor_copy(bbf, r2), s=P.sv)
        row_done = P.sv.at()
        wb0 = [row_done]
        if bc_done is not None:
            wb0.append(bc_done)
        bc4 = C["bc4"]
        em.op("tensor", lambda t, bc4=bc4: t.matmul(
            P.PB[:, 0:512], bc4[0:4, :], rbf[0:4, 0:512], start=True, stop=True), w=wb0, s=P.st)
        em.op("tensor", lambda t, bc4=bc4: t.matmul(
            P.PB[:, 512:1024], bc4[0:4, :], rbf[0:4, 512:1024], start=True, stop=True), s=P.st)
        tw = P.st.at()
        em.op("scalar", lambda s, tb=tb: s.copy(AB[:, tb * TB:(tb + 1) * TB], P.PB[:, 0:512]),
              w=[tw], s=P.ss)
        em.op("scalar", lambda s, tb=tb: s.copy(BB[:, tb * TB:(tb + 1) * TB], P.PB[:, 512:1024]),
              s=P.ss)
        bc_done = P.ss.at()
    pairs = [(c, tb) for c in range(DC) for tb in range(NTB)]
    for i in range(0, len(pairs), 2):
        for j, (c, tb) in enumerate(pairs[i:i + 2]):
            w = [bc_done] if (c == 0 and tb == 0) else ()
            em.op("vector", lambda v, c=c, tb=tb, j=j: v.tensor_tensor(
                t1[:, j * TB:(j + 1) * TB], P.XT[c][:, tb * TB:(tb + 1) * TB],
                AB[:, tb * TB:(tb + 1) * TB], ALU.mult), w=w)
        for j, (c, tb) in enumerate(pairs[i:i + 2]):
            em.op("vector", lambda v, c=c, tb=tb, j=j: v.tensor_tensor(
                P.XH[c][:, tb * TB:(tb + 1) * TB], t1[:, j * TB:(j + 1) * TB],
                BB[:, tb * TB:(tb + 1) * TB], ALU.add), s=P.sv)
    vfence(em, P, P.XHBIG[:, DC * T - 256:DC * T], s_=P.sv)
    P.xh_ready = P.sv.at()


def make_ln_scratch(em):
    return {
        "xc": sbuf(em, "ln_xc", [128, 1024], BF16),
        "xq": sbuf(em, "ln_xq", [128, 1024], BF16),
        "rows": sbuf(em, "ln_rows", [1, 2048], F32),
        "rbf": sbuf(em, "ln_rbf", [4, 1024], BF16),
        "AB": sbuf(em, "ln_AB", [128, T], BF16),
        "BB": sbuf(em, "ln_BB", [128, T], BF16),
        "t1": sbuf(em, "ln_t1", [128, 1024], F32),
    }


def emit_ffn(em, P, C, W, layer):
    with contextlib.ExitStack() as fscope:
        outer, em.stack = em.stack, fscope
        try:
            emit_ffn_body(em, P, C, W, layer)
        finally:
            em.stack = outer


def emit_ffn_body(em, P, C, W, layer):
    STAGE = os.environ.get("KSTAGE", "all")
    if layer == 0:
        E, NCHUNK, HCS, SLABS, HTOT = E0, 22, HC0, SLAB0, H0
        act_fn, gated = AF.Silu, True
    else:
        E, NCHUNK, HCS, SLABS, HTOT = E1, 16, [128] * 16, SLAB1, F1
        act_fn, gated = AF.Gelu, False
    MAXCH = max(j1 - j0 for j0, j1 in SLABS)
    hlen_max = MAXCH * 128
    idn = C["idn"]
    esel_d = C["esel0_d"] if layer == 0 else C["esel1_d"]
    esel = sbuf(em, "esel", [E, E * 128], BF16)
    sesel = em.sem("sesel")
    em.op("gpsimd", lambda g, esel_d=esel_d: g.dma_start(out=esel[:], in_=esel_d),
          s=sesel, k=16)
    esel_w = sesel.at()
    wg, wu, wd = (W["gate_w"], W["up_w"], W["down_w"]) if gated else (W["w1"], None, W["w2"])
    wgb, wub, wdb = (W["gate_b"], W["up_b"], W["down_b"]) if gated else (W["b1"], None, W["b2"])
    wrouter = W["router_w"]

    gbT = sbuf(em, "gbT", [128, NCHUNK * E], F32)
    ubT = sbuf(em, "ubT", [128, NCHUNK * E], F32) if gated else None
    db_sb = sbuf(em, "db_sb", [E, D], BF16)
    rw_sb = sbuf(em, "rw_sb", [128, DC * E], BF16)
    mtT = sbuf(em, "mtT", [E, T], BF16)
    wbc = sbuf(em, "wbc", [128, T], BF16)
    plf = sbuf(em, "plf", [128, NT * E], F32)
    r_m1 = sbuf(em, "r_m1", [128, NT], F32)
    r_m2 = sbuf(em, "r_m2", [128, NT], F32)
    r_sc1 = sbuf(em, "r_sc1", [128, NT * E], F32)
    r_sc2 = sbuf(em, "r_sc2", [128, NT * E], F32)
    r_mt = sbuf(em, "r_mt", [128, NT * E], F32)
    r_den = sbuf(em, "r_den", [128, NT], F32)

    stab = em.sem("stab")
    em.op("gpsimd", lambda g: g.dma_start(out=db_sb[:], in_=wdb), s=stab, k=16)
    em.op("gpsimd", lambda g: g.dma_start(
        out=rw_sb[:].rearrange("k (c e) -> k c e", c=DC),
        in_=wrouter.rearrange("(c k) e -> k c e", c=DC)), s=stab, k=16)
    small_w = stab.at()
    if STAGE == "tables0":
        return

    with contextlib.ExitStack() as scope:
        em2_stack, em.stack = em.stack, scope
        gb_raw = sbuf(em, "gb_raw", [E, NCHUNK * 128], F32)
        ub_raw = sbuf(em, "ub_raw", [E, NCHUNK * 128], F32) if gated else None
        sbr = em.sem("sbraw")
        if HTOT < NCHUNK * 128:
            em.op("vector", lambda v: v.memset(gb_raw[:], 0.0), s=P.sv)
            if gated:
                em.op("vector", lambda v: v.memset(ub_raw[:], 0.0), s=P.sv)
        mz = P.sv.at()
        em.op("sync", lambda y: y.dma_start(out=gb_raw[0:E, 0:HTOT], in_=wgb), w=[mz], s=sbr, k=16)
        if gated:
            em.op("sync", lambda y: y.dma_start(out=ub_raw[0:E, 0:HTOT], in_=wub), s=sbr, k=16)
        braww = sbr.at()
        for bi in range(2 if gated else 1):
            src = gb_raw if bi == 0 else ub_raw
            dstT = gbT if bi == 0 else ubT
            for j in range(NCHUNK):
                em.op("tensor", lambda t, j=j, src=src: t.transpose(
                    P.PM[0:128, j * E:(j + 1) * E], src[0:E, j * 128:(j + 1) * 128], idn[0:E, 0:E]),
                    w=([braww, (P.ss, P.ss.n)] if j == 0 else []),
                    s=(P.st if j == NCHUNK - 1 else None))
            em.op("scalar", lambda s, dstT=dstT: s.copy(dstT[:], P.PM[:, 0:NCHUNK * E]),
                  w=[P.st.at()], s=P.ss)
        biasT_pe = P.st.at()
        biasT_done = P.ss.at()
        if STAGE == "tables":
            em.stack = em2_stack
            return
        ln_sb = make_ln_scratch(em)
        C["dbg_ln_sb"] = ln_sb
        emit_ln(em, P, C, ln_sb)
        em.stack = em2_stack
    xhw = P.xh_ready
    if STAGE == "ln":
        for c in range(DC):
            em.op("vector", lambda v, c=c: v.tensor_copy(P.XT[c], P.XH[c]),
                  w=[xhw] if c == 0 else (), s=P.sv)
        return
    alias_gate = [xhw, biasT_pe, biasT_done]

    for ti in range(NT):
        for c in range(DC):
            w = [xhw, small_w, biasT_done] if (ti == 0 and c == 0) else []
            em.op("tensor", lambda t, ti=ti, c=c: t.matmul(
                P.PM[0:128, ti * E:(ti + 1) * E],
                P.XH[c][:, ti * 128:(ti + 1) * 128],
                rw_sb[:, c * E:(c + 1) * E],
                start=(c == 0), stop=(c == DC - 1)),
                w=w, s=None)
    em.op("tensor", lambda t: t.matmul(
        P.PM[0:128, 300:300 + E], P.XH[0][:, 0:128], rw_sb[:, 0:E], start=True, stop=True), s=P.st)
    em.op("scalar", lambda s: s.copy(plf[:], P.PM[:, 0:NT * E]), w=[P.st.at()], s=None)
    sfence(em, P, plf[:], s_=P.ss)
    plw = P.ss.at()

    pl3 = plf[:].rearrange("p (t e) -> p t e", e=E)
    m1b = r_m1[:].rearrange("p (t o) -> p t o", o=1).broadcast_to([128, NT, E])
    m2b = r_m2[:].rearrange("p (t o) -> p t o", o=1).broadcast_to([128, NT, E])
    em.op("vector", lambda v: v.reduce_max(r_m1[:], pl3, axis=AX.X), w=[plw])
    vfence(em, P, r_m1[:])
    if layer == 0:
        em.op("vector", lambda v: v.tensor_tensor(
            r_sc1[:].rearrange("p (t e) -> p t e", e=E), pl3, m1b, ALU.is_ge))
        vfence(em, P, r_sc1[:])
        em.op("vector", lambda v: v.scalar_tensor_tensor(r_sc2[:], r_sc1[:], NEG, plf[:], ALU.mult, ALU.add))
        vfence(em, P, r_sc2[:])
        em.op("vector", lambda v: v.reduce_max(r_m2[:], r_sc2[:].rearrange("p (t e) -> p t e", e=E), axis=AX.X))
        vfence(em, P, r_m2[:])
        em.op("vector", lambda v: v.tensor_tensor(
            r_sc2[:].rearrange("p (t e) -> p t e", e=E), pl3, m2b, ALU.is_ge))
        vfence(em, P, r_sc2[:])
        em.op("vector", lambda v: v.tensor_tensor(
            r_sc1[:].rearrange("p (t e) -> p t e", e=E), pl3, m1b, ALU.subtract))
        vfence(em, P, r_sc1[:], s_=P.sv)
        em.op("scalar", lambda s: s.activation(r_sc1[:], r_sc1[:], AF.Exp), w=[P.sv.at()], s=None)
        sfence(em, P, r_sc1[:], s_=P.ss)
        em.op("vector", lambda v: v.tensor_tensor(r_sc1[:], r_sc1[:], r_sc2[:], ALU.mult),
              w=[P.ss.at()])
        vfence(em, P, r_sc1[:])
        em.op("vector", lambda v: v.reduce_sum(r_den[:], r_sc1[:].rearrange("p (t e) -> p t e", e=E), axis=AX.X))
        vfence(em, P, r_den[:])
        em.op("vector", lambda v: v.reciprocal(r_m2[:], r_den[:]))
        vfence(em, P, r_m2[:])
        em.op("vector", lambda v: v.tensor_scalar(r_m2[:], r_m2[:], 0.5, None, ALU.mult))
        vfence(em, P, r_m2[:])
        em.op("vector", lambda v: v.tensor_tensor(
            r_mt[:].rearrange("p (t e) -> p t e", e=E), r_sc1[:].rearrange("p (t e) -> p t e", e=E),
            m2b, ALU.mult))
        vfence(em, P, r_mt[:], s_=P.sv)
    else:
        em.op("vector", lambda v: v.tensor_tensor(
            r_sc1[:].rearrange("p (t e) -> p t e", e=E), pl3, m1b, ALU.subtract))
        vfence(em, P, r_sc1[:], s_=P.sv)
        em.op("scalar", lambda s: s.activation(r_sc1[:], r_sc1[:], AF.Exp), w=[P.sv.at()], s=None)
        sfence(em, P, r_sc1[:], s_=P.ss)
        em.op("vector", lambda v: v.reduce_sum(r_den[:], r_sc1[:].rearrange("p (t e) -> p t e", e=E), axis=AX.X),
              w=[P.ss.at()])
        vfence(em, P, r_den[:])
        em.op("vector", lambda v: v.reciprocal(r_m2[:], r_den[:]))
        vfence(em, P, r_m2[:])
        em.op("vector", lambda v: v.tensor_scalar(r_m2[:], r_m2[:], 0.5, None, ALU.mult))
        vfence(em, P, r_m2[:])
        em.op("vector", lambda v: v.tensor_tensor(
            r_sc2[:].rearrange("p (t e) -> p t e", e=E), pl3, m1b, ALU.is_ge))
        vfence(em, P, r_sc2[:])
        em.op("vector", lambda v: v.tensor_tensor(
            r_mt[:].rearrange("p (t e) -> p t e", e=E), r_sc2[:].rearrange("p (t e) -> p t e", e=E),
            m2b, ALU.mult))
        vfence(em, P, r_mt[:], s_=P.sv)
    mt_done = P.sv.at()
    if STAGE == "dbg_direct":
        dst = P.dbg_out
        em.op("sync", lambda y: y.dma_start(out=dst[0:128, 0:NT * E], in_=plf[:]), w=[plw], s=P.syd, k=16)
        em.op("gpsimd", lambda y: y.dma_start(out=dst[128:256, 0:T], in_=P.XH[0]), w=[xhw], s=P.sgd, k=16)
        em.op("sync", lambda y: y.dma_start(out=dst[256:384, 0:NT * E], in_=r_mt[:]), w=[mt_done], s=P.syd, k=16)
        em.op("gpsimd", lambda y: y.dma_start(out=dst[384:512, 0:DC * E], in_=rw_sb[:]), w=[small_w], s=P.sgd, k=16)
        em.op("sync", lambda y: y.dma_start(out=dst[512:640, 0:T], in_=P.XT[0][:]), w=[xhw], s=P.syd, k=16)
        ln_sb2 = C["dbg_ln_sb"]
        em.op("gpsimd", lambda y: y.dma_start(out=dst[640:768, 0:T], in_=ln_sb2["AB"][:]), w=[xhw], s=P.sgd, k=16)
        em.op("gpsimd", lambda y: y.dma_start(out=dst[768:896, 0:T], in_=ln_sb2["BB"][:]), w=[xhw], s=P.sgd, k=16)
        em.barrier_wait("sync", [P.syd.at()])
        em.barrier_wait("gpsimd", [P.sgd.at()])
        P.dbg_done = True
        return
    if STAGE == "routing":
        return

    per_round = 512 // 128
    prev_evac = None
    for r0 in range(0, NT, per_round):
        for k in range(per_round):
            ti = r0 + k
            w = [mt_done] if (r0 == 0 and k == 0) else ([prev_evac] if (k == 0 and prev_evac) else [])
            em.op("tensor", lambda t, ti=ti, k=k: t.transpose(
                P.PM[0:E, k * 128:(k + 1) * 128], r_mt[:, ti * E:(ti + 1) * E], idn[:]),
                w=w, s=(P.st if k == per_round - 1 else None))
        em.op("scalar", lambda s, r0=r0: s.copy(
            mtT[0:E, r0 * 128:(r0 + per_round) * 128], P.PM[0:E, 0:per_round * 128]),
            w=[P.st.at()], s=P.ss)
        prev_evac = P.ss.at()
    mtT_done = P.ss.at()
    if STAGE == "dbg_mtt":
        em.op("vector", lambda v: v.tensor_copy(P.XTBIG[0:E, 0:T], mtT[0:E, :]), w=[mtT_done], s=P.sv)
        em.op("vector", lambda v: v.tensor_copy(P.XTBIG[:, T:T + NT * E], r_mt[:]), s=P.sv)
        em.op("vector", lambda v: v.tensor_copy(P.XTBIG[:, 2 * T:2 * T + NT * E], plf[:]), s=P.sv)
        em.op("vector", lambda v: v.tensor_copy(P.XTBIG[:, 3 * T:3 * T + NT], r_m2[:]), s=P.sv)
        em.op("vector", lambda v: v.tensor_copy(P.XTBIG[:, 6 * T:6 * T + DC * E], rw_sb[:]), s=P.sv)
        em.op("vector", lambda v: v.tensor_copy(P.XTBIG[:, 7 * T:8 * T], P.XH[0][:]), s=P.sv)
        em.op("vector", lambda v: v.tensor_copy(P.XTBIG[:, 4 * T:4 * T + NT * E], r_sc1[:]), s=P.sv)
        em.op("vector", lambda v: v.tensor_copy(P.XTBIG[:, 5 * T:5 * T + NT * E], r_sc2[:]), s=P.sv)
        return
    if STAGE == "mtt":
        return

    gw_buf = [sbuf(em, f"gw{i}", [128, DC * hlen_max], BF16) for i in range(2)]
    uw_buf = [sbuf(em, f"uw{i}", [128, DC * hlen_max], BF16) for i in range(2)] if gated else [None, None]
    dw_buf = [sbuf(em, f"dw{i}", [128, MAXCH * D], BF16) for i in range(2)]
    gu_g = sbuf(em, "gu_g", [128, 1024], BF16)
    gu_u = sbuf(em, "gu_u", [128, 1024], BF16) if gated else None
    hs_buf = sbuf(em, "hs", [128, MAXCH * TB], BF16)

    slabs = [(e, j0, j1) for e in range(E) for (j0, j1) in SLABS]
    if STAGE.startswith("slab"):
        slabs = slabs[: int(STAGE[4:])]
    sgw = [em.sem("sgw0"), em.sem("sgw1")]
    buf_free_t = [None, None]
    wready = [None, None]
    wbc_used = None
    hs_free = None
    py_evac = [(P.sv, 0), (P.sv, 0)]
    gu_evac = [(P.ss, 0), (P.ss, 0)]
    gu_used = [(P.sv, 0), (P.sv, 0)]
    last_e = None
    wbc_ready = None

    for si, (e, j0, j1) in enumerate(slabs):
        buf = si % 2
        nch = j1 - j0
        hlo, hhi = j0 * 128, min(j1 * 128, HTOT)
        hlen = hhi - hlo
        wfree = list(alias_gate) if si < 2 else [buf_free_t[buf]]
        em.op("gpsimd", lambda g, e=e, hlo=hlo, hhi=hhi, hlen=hlen, buf=buf: g.dma_start(
            out=gw_buf[buf][:, 0:DC * hlen].rearrange("k (c h) -> k c h", c=DC),
            in_=wg[e, :, hlo:hhi].rearrange("(c k) h -> k c h", c=DC)),
            w=wfree, s=sgw[buf], k=16)
        if gated:
            em.op("gpsimd", lambda g, e=e, hlo=hlo, hhi=hhi, hlen=hlen, buf=buf: g.dma_start(
                out=uw_buf[buf][:, 0:DC * hlen].rearrange("k (c h) -> k c h", c=DC),
                in_=wu[e, :, hlo:hhi].rearrange("(c k) h -> k c h", c=DC)),
                s=sgw[buf], k=16)
        for jj in range(nch):
            j = j0 + jj
            nr = HCS[j]
            em.op("gpsimd", lambda g, e=e, j=j, jj=jj, nr=nr, buf=buf: g.dma_start(
                out=dw_buf[buf][0:nr, jj * D:(jj + 1) * D],
                in_=wd[e, j * 128:j * 128 + nr, :]), s=sgw[buf], k=16)
        wready[buf] = sgw[buf].at()

        if e != last_e:
            for tb in range(NTB):
                w = [mtT_done, esel_w] if tb == 0 else [(P.ss, P.ss.n)]
                if tb == 0 and wbc_used is not None:
                    w.append(wbc_used)
                em.op("tensor", lambda t, e=e, tb=tb: t.matmul(
                    P.PM[:, 0:512], esel[0:E, e * 128:(e + 1) * 128],
                    mtT[0:E, tb * TB:(tb + 1) * TB], start=True, stop=True),
                    w=w, s=P.st)
                em.op("scalar", lambda s, tb=tb: s.copy(
                    wbc[:, tb * TB:(tb + 1) * TB], P.PM[:, 0:512]), w=[P.st.at()], s=P.ss)
            wbc_ready = P.ss.at()
            last_e = e

        for tb in range(NTB):
            for jj in range(nch):
                j = j0 + jj
                nr = HCS[j]
                half = jj % 2
                pg = P.PA[:, half * 512:(half + 1) * 512]
                pu = P.PB[:, half * 512:(half + 1) * 512] if gated else None
                w = [gu_evac[half]]
                if tb == 0 and jj == 0:
                    w += [wready[buf], xhw]
                for c in range(DC):
                    em.op("tensor", lambda t, c=c, jj=jj, nr=nr, tb=tb, buf=buf, pg=pg, hlen=hlen: t.matmul(
                        pg[0:nr, :], gw_buf[buf][:, c * hlen + jj * 128:c * hlen + jj * 128 + nr],
                        P.XH[c][:, tb * TB:(tb + 1) * TB],
                        start=(c == 0), stop=(c == DC - 1)),
                        w=w if c == 0 else (), s=(P.st if (c == DC - 1 and not gated) else None))
                if gated:
                    for c in range(DC):
                        em.op("tensor", lambda t, c=c, jj=jj, nr=nr, tb=tb, buf=buf, pu=pu, hlen=hlen: t.matmul(
                            pu[0:nr, :], uw_buf[buf][:, c * hlen + jj * 128:c * hlen + jj * 128 + nr],
                            P.XH[c][:, tb * TB:(tb + 1) * TB],
                            start=(c == 0), stop=(c == DC - 1)),
                            s=(P.st if c == DC - 1 else None))
                mmw = P.st.at()
                em.op("scalar", lambda s, nr=nr, half=half, j=j, e=e, pg=pg: s.activation(
                    gu_g[0:nr, half * 512:(half + 1) * 512], pg[0:nr, :], act_fn,
                    bias=gbT[0:nr, j * E + e:j * E + e + 1], scale=1.0),
                    w=[mmw, gu_used[half]], s=(None if gated else P.ss))
                if gated:
                    em.op("scalar", lambda s, nr=nr, half=half, j=j, e=e, pu=pu: s.activation(
                        gu_u[0:nr, half * 512:(half + 1) * 512], pu[0:nr, :], AF.Identity,
                        bias=ubT[0:nr, j * E + e:j * E + e + 1], scale=1.0), s=P.ss)
                gu_evac[half] = P.ss.at()
                wv = [gu_evac[half], wbc_ready]
                if jj == 0 and hs_free is not None:
                    wv.append(hs_free)
                if gated:
                    em.op("vector", lambda v, nr=nr, half=half: v.tensor_tensor(
                        gu_g[0:nr, half * 512:(half + 1) * 512],
                        gu_g[0:nr, half * 512:(half + 1) * 512],
                        gu_u[0:nr, half * 512:(half + 1) * 512], ALU.mult), w=wv)
                    em.op("vector", lambda v, nr=nr, half=half, jj=jj, tb=tb: v.tensor_tensor(
                        hs_buf[0:nr, jj * TB:(jj + 1) * TB],
                        gu_g[0:nr, half * 512:(half + 1) * 512],
                        wbc[0:nr, tb * TB:(tb + 1) * TB], ALU.mult), s=P.sv)
                else:
                    em.op("vector", lambda v, nr=nr, half=half, jj=jj, tb=tb: v.tensor_tensor(
                        hs_buf[0:nr, jj * TB:(jj + 1) * TB],
                        gu_g[0:nr, half * 512:(half + 1) * 512],
                        wbc[0:nr, tb * TB:(tb + 1) * TB], ALU.mult), w=wv, s=P.sv)
                gu_used[half] = P.sv.at()
            hs_done = P.sv.at()
            wbc_used = hs_done
            for cc in range(DC):
                half = cc % 2
                py = P.PY[:, half * 512:(half + 1) * 512]
                first = True
                w = [hs_done, py_evac[half]]
                if si == 0:
                    em.op("tensor", lambda t, cc=cc, tb=tb, py=py: t.matmul(
                        py, db_sb[0:E, cc * 128:(cc + 1) * 128], mtT[0:E, tb * TB:(tb + 1) * TB],
                        start=True, stop=False), w=w + [small_w, mtT_done], s=None)
                    first = False
                    w = []
                for jj in range(nch):
                    j = j0 + jj
                    nr = HCS[j]
                    em.op("tensor", lambda t, jj=jj, nr=nr, cc=cc, buf=buf, py=py, st=(first and jj == 0), sp=(jj == nch - 1): t.matmul(
                        py, dw_buf[buf][0:nr, jj * D + cc * 128:jj * D + cc * 128 + 128],
                        hs_buf[0:nr, jj * TB:(jj + 1) * TB],
                        start=st, stop=sp),
                        w=w if jj == 0 else (), s=(P.st if jj == nch - 1 else None))
                    w = []
                    first = False
                mmw = P.st.at()
                em.op("vector", lambda v, cc=cc, tb=tb, py=py: v.tensor_tensor(
                    P.XT[cc][:, tb * TB:(tb + 1) * TB], P.XT[cc][:, tb * TB:(tb + 1) * TB],
                    py, ALU.add), w=[mmw], s=P.sv)
                py_evac[half] = P.sv.at()
            hs_free = P.st.at()
        buf_free_t[buf] = P.st.at()
    P.ffn_done = P.sv.at()


def emit_attn(em, P, C, W, layer, xsrc):
    nc = em.nc
    idn_bf = C["idn_bf"]
    bc4 = C["bc4"]
    QB = 4
    scale = 1.0 / 8.0

    if not hasattr(P, "kcoll_in"):
        P.kcoll_in = nc.dram_tensor("kcoll_in", [D, T], BF16).ap()
        P.kcoll_out = nc.dram_tensor("kcoll_out", [2 * D, T], BF16).ap()
        P.vcoll_in = nc.dram_tensor("vcoll_in", [T, NH * 65], BF16).ap()
        P.vcoll_out = nc.dram_tensor("vcoll_out", [2 * T, NH * 65], BF16).ap()
    kcoll_in, kcoll_out = P.kcoll_in, P.kcoll_out
    vcoll_in, vcoll_out = P.vcoll_in, P.vcoll_out

    with contextlib.ExitStack() as scope:
        outer, em.stack = em.stack, scope
        raw = [sbuf(em, f"araw{i}", [128, 3072 if layer == 0 else 1024], BF16) for i in range(2)]
        rotQ = sbuf(em, "rotQ", [128, DC * T], BF16)
        with contextlib.ExitStack() as scope_ln:
            em.stack = scope_ln
            ln_sb = make_ln_scratch(em)
            emit_ln(em, P, C, ln_sb)
            em.stack = scope
        xhw = P.xh_ready

        scope_w = contextlib.ExitStack()
        em.stack = scope_w
        swq = em.sem("swq")
        if layer == 0:
            wq_sb = sbuf(em, "wq_sb", [128, DC * 1024], BF16)
            wk_sb = sbuf(em, "wk_sb", [128, DC * 1024], BF16)
            wv_sb = sbuf(em, "wv_sb", [128, DC * 1024], BF16)
            wqkv = W["W_qkv"]
            raw_used = [None, None]
            for c in range(DC):
                b = c % 2
                wld = [raw_used[b]] if raw_used[b] is not None else [xhw]
                em.op("gpsimd", lambda g, c=c, b=b: g.dma_start(
                    out=raw[b][:], in_=wqkv[c * 128:(c + 1) * 128, :]), w=wld, s=swq, k=16)
                dw = swq.at()
                for wi, dst in enumerate((wq_sb, wk_sb, wv_sb)):
                    em.op("vector", lambda v, c=c, b=b, wi=wi, dst=dst: v.tensor_copy(
                        dst[:, c * 1024:(c + 1) * 1024],
                        raw[b][:].rearrange("p (n s) -> p n s", s=3)[:, :, wi]),
                        w=[dw] if wi == 0 else (), s=(P.sv if wi == 2 else None))
                raw_used[b] = P.sv.at()
            qkvw_done = P.sv.at()
            wsel = {"q": wq_sb, "k": wk_sb, "v": wv_sb}
        else:
            wT = sbuf(em, "wT", [128, 64 * 128], BF16)
            wTk = sbuf(em, "wTk", [128, 64 * 128], BF16)
            raw_used = [None, None]
            tdone = [None, None]

            def prep_matrix(wnm, rope_perm, gate, dstT):
                wm = W[wnm]
                for fc in range(DC):
                    b = fc % 2
                    wld = list(gate) if fc == 0 else []
                    if raw_used[b] is not None:
                        wld.append(raw_used[b])
                    if rope_perm:
                        wv4 = wm.rearrange("(h dk2 two) d -> h two dk2 d", dk2=32, two=2)
                        for pi, (hl, sgn) in enumerate(
                                ((0, 0), (0, 1), (1, 0), (1, 1))):
                            src = wv4[2 * fc + hl, sgn]
                            dst = raw[b][hl * 64 + sgn * 32: hl * 64 + sgn * 32 + 32, :]
                            em.op("gpsimd", lambda g, src=src, dst=dst: g.dma_start(
                                out=dst, in_=src), w=wld if pi == 0 else [],
                                s=swq, k=16)
                    else:
                        src = wm[fc * 128:(fc + 1) * 128, :]
                        em.op("gpsimd", lambda g, src=src, b=b: g.dma_start(
                            out=raw[b][:], in_=src), w=wld, s=swq, k=16)
                    dw = swq.at()
                    for dc in range(DC):
                        pm = P.PM[:, 0:128] if dc % 2 == 0 else P.P8[:, 0:128]
                        wv_ = [dw] if dc == 0 else []
                        if tdone[dc % 2] is not None:
                            wv_.append(tdone[dc % 2])
                        em.op("tensor", lambda t, b=b, dc=dc, pm=pm: t.matmul(
                            pm, raw[b][:, dc * 128:(dc + 1) * 128], idn_bf[:],
                            start=True, stop=True), w=wv_, s=P.st)
                        tw = P.st.at()
                        em.op("scalar", lambda sc, dc=dc, fc=fc, pm=pm, dstT=dstT: sc.copy(
                            dstT[:, (dc * DC + fc) * 128:(dc * DC + fc) * 128 + 128], pm),
                            w=[tw], s=P.ss)
                        tdone[dc % 2] = P.ss.at()
                    raw_used[b] = P.ss.at()
                return P.ss.at()
            qkvw_done = None
            wsel = {"q": wT, "k": wTk, "v": wT}
            cosr = sbuf(em, "cosr", [32, T], F32)
            sinr = sbuf(em, "sinr", [32, T], F32)
            srt = em.sem("srt")
            cd, sd_ = C["cos_d"], C["sin_d"]
            em.op("sync", lambda y, cd=cd: y.dma_start(out=cosr[:], in_=cd),
                  w=[xhw], s=srt, k=16)
            em.op("sync", lambda y, sd_=sd_: y.dma_start(out=sinr[:], in_=sd_), s=srt, k=16)
            rope_w = srt.at()
            rp1 = sbuf(em, "rp1", [32, 512], F32)
            rp2 = sbuf(em, "rp2", [32, 512], F32)
            rp3 = sbuf(em, "rp3", [32, 512], F32)
            rp4 = sbuf(em, "rp4", [32, 512], F32)

        def wslice(which, dc, fc):
            if layer == 0:
                return wsel[which][:, dc * 1024 + fc * 128: dc * 1024 + (fc + 1) * 128]
            return wsel[which][:, (dc * DC + fc) * 128:(dc * DC + fc) * 128 + 128]

        def vslice(dc, fb):
            if layer == 0:
                return wv_sb[:, dc * 1024 + fb * 512: dc * 1024 + (fb + 1) * 512]
            return wT[:, (dc * DC + 4 * fb) * 128:(dc * DC + 4 * fb + 4) * 128]

        krow = [sbuf(em, f"krow{i}", [128, T], BF16) for i in range(2)]
        vrow = [sbuf(em, f"vrow{i}", [128, NH * 65], BF16) for i in range(2)]
        skst = [em.sem("skst0"), em.sem("skst1")]
        svst = [em.sem("svst0"), em.sem("svst1")]
        em.op("gpsimd", lambda g: g.memset(vrow[0][:], 1.0), s=P.sg)
        em.op("gpsimd", lambda g: g.memset(vrow[1][:], 1.0), s=P.sg)
        vrow_ones = P.sg.at()

        def emit_rope(dst_ap, psrc, tb, w_first, s_last):
            cs = cosr[0:32, tb * TB:(tb + 1) * TB]
            sn = sinr[0:32, tb * TB:(tb + 1) * TB]
            qe = [psrc[0:32, :], psrc[64:96, :]]
            qo = [psrc[32:64, :], psrc[96:128, :]]
            rp = [rp1, rp2, rp3, rp4]
            for pass_ in range(2):
                f1 = cs if pass_ == 0 else sn
                f2 = sn if pass_ == 0 else cs
                alu = ALU.subtract if pass_ == 0 else ALU.add
                for hh in range(2):
                    em.op("vector", lambda v, hh=hh, f1=f1: v.tensor_tensor(
                        rp[2 * hh][:], qe[hh], f1, ALU.mult),
                        w=w_first if (pass_ == 0 and hh == 0) else ())
                    em.op("vector", lambda v, hh=hh, f2=f2: v.tensor_tensor(
                        rp[2 * hh + 1][:], qo[hh], f2, ALU.mult))
                for hh in range(2):
                    b0 = hh * 64 + (0 if pass_ == 0 else 32)
                    em.op("vector", lambda v, dst_ap=dst_ap, b0=b0, hh=hh, alu=alu: v.tensor_tensor(
                        dst_ap[b0:b0 + 32, :], rp[2 * hh][:], rp[2 * hh + 1][:], alu),
                        s=(s_last if pass_ == 1 and hh == 1 else None))

        qbank = [P.PA[:, 0:512], P.PA[:, 512:1024], P.PB[:, 0:512], P.PB[:, 512:1024]]
        pa_ev4q = [(P.ss, 0)] * 4
        kst_hist = [None, None]
        proj_done = {}
        if layer == 1:
            wT_w = {"q": prep_matrix("wq", True, [xhw], wT),
                    "k": prep_matrix("wk", True, [xhw], wTk)}
        for which in ("q", "k"):
            for fc in range(DC):
                b = fc % 2
                for tb in range(NTB):
                    pq = qbank[tb]
                    wv_ = [pa_ev4q[tb]]
                    if fc == 0 and tb == 0:
                        if layer == 1:
                            wv_.append(wT_w[which])
                        if which == "q":
                            wv_ += [xhw]
                            if layer == 0:
                                wv_.append(qkvw_done)
                            if layer == 1:
                                wv_.append(rope_w)
                    for dc in range(DC):
                        em.op("tensor", lambda t, which=which, dc=dc, fc=fc, tb=tb, pq=pq: t.matmul(
                            pq, wslice(which, dc, fc), P.XH[dc][:, tb * TB:(tb + 1) * TB],
                            start=(dc == 0), stop=(dc == DC - 1)),
                            w=wv_ if dc == 0 else (), s=(P.st if dc == DC - 1 else None))
                    mmw = P.st.at()
                    dst = rotQ[:, fc * T + tb * TB: fc * T + (tb + 1) * TB] if which == "q" \
                        else krow[b][:, tb * TB:(tb + 1) * TB]
                    kst_w = [kst_hist[b]] if (which == "k" and tb == 0 and kst_hist[b] is not None) else []
                    if layer == 1:
                        emit_rope(dst, pq, tb, [mmw] + kst_w, P.sv)
                        pa_ev4q[tb] = P.sv.at()
                        done_by = P.sv
                    else:
                        em.op("scalar", lambda sc, dst=dst, pq=pq: sc.copy(dst, pq),
                              w=[mmw] + kst_w, s=P.ss)
                        pa_ev4q[tb] = P.ss.at()
                        done_by = P.ss
                if which == "k":
                    em.op("sync", lambda y, fc=fc, b=b: y.dma_start(
                        out=kcoll_in[fc * 128:(fc + 1) * 128, :], in_=krow[b][:]),
                        w=[done_by.at()], s=skst[b], k=16)
                    kst_hist[b] = skst[b].at()
            if which == "q":
                qdone = done_by.at()
            proj_done[which] = P.st.at()
        kst_all = [skst[0].at(), skst[1].at()]

        pb_ev = [pa_ev4q[2], pa_ev4q[3]]
        if layer == 1:
            wTv_w = prep_matrix("wv", False, [proj_done["q"]], wT)
        vst_hist = [None, None]
        for ti in range(NT):
            b = ti % 2
            for fb in range(2):
                half = fb
                pv = P.PB[:, half * 512:(half + 1) * 512]
                wv_ = [pb_ev[half]]
                if ti == 0 and fb == 0:
                    wv_.append(vrow_ones)
                    if layer == 1:
                        wv_.append(wTv_w)
                if fb == 0 and vst_hist[b] is not None:
                    wv_.append(vst_hist[b])
                for dc in range(DC):
                    em.op("tensor", lambda t, ti=ti, fb=fb, dc=dc, pv=pv: t.matmul(
                        pv, P.XH[dc][:, ti * 128:(ti + 1) * 128], vslice(dc, fb),
                        start=(dc == 0), stop=(dc == DC - 1)),
                        w=wv_ if dc == 0 else (), s=(P.st if dc == DC - 1 else None))
                mmw = P.st.at()
                em.op("scalar", lambda sc, b=b, fb=fb, pv=pv: sc.copy(
                    vrow[b][:].rearrange("p (h x) -> p h x", x=65)[:, fb * 8:(fb + 1) * 8, 0:64],
                    pv.rearrange("p (h dk) -> p h dk", dk=64)),
                    w=[mmw], s=P.ss)
                pb_ev[half] = P.ss.at()
            em.op("sync", lambda y, ti=ti, b=b: y.dma_start(
                out=vcoll_in[ti * 128:(ti + 1) * 128, :], in_=vrow[b][:]),
                w=[P.ss.at()], s=svst[b], k=16)
            vst_hist[b] = svst[b].at()
        vst_all = [svst[0].at(), svst[1].at()]
        scope_w.close()
        em.stack = scope

        ccs = em.sem("ccs")
        groups = [[0, 1], [2, 3], [4, 5], [6, 7]]
        em.op("gpsimd", lambda g: g.collective_compute(
            "AllGather", ALU.bypass, replica_groups=groups,
            ins=[kcoll_in], outs=[kcoll_out]), w=kst_all, s=ccs, k=1)
        em.op("gpsimd", lambda g: g.collective_compute(
            "AllGather", ALU.bypass, replica_groups=groups,
            ins=[vcoll_in], outs=[vcoll_out]), w=vst_all, s=ccs, k=1)
        cc_done = ccs.at()

        xtb_bf = P.XTBIG[:].bitcast(BF16)
        vx = sbuf(em, "vx", [128, 65 * NH], BF16)

        def vhat(g):
            if g < 31:
                return xtb_bf[:, g * 1040:(g + 1) * 1040]
            return vx[:]
        svh = em.sem("svh")
        stash_w = [getattr(P, "stash_done", None)] if getattr(P, "stash_done", None) else []
        for g in range(KT):
            src_row = (g % 2) * T + (g // 2) * 128
            em.op("sync", lambda y, g=g, src_row=src_row: y.dma_start(
                out=vhat(g), in_=vcoll_out[src_row:src_row + 128, :]),
                w=[cc_done] + stash_w if g == 0 else (), s=svh, k=16)
        vhat_w = svh.at()

        def attnT(qb, fc):
            return P.XHBIG[:, qb * (DC * TB) + fc * TB: qb * (DC * TB) + (fc + 1) * TB]
        kh = [sbuf(em, f"kh{i}", [64, S], BF16) for i in range(2)]
        skh = [em.sem("skh0"), em.sem("skh1")]
        kh_used = [None, None]
        qhat = [sbuf(em, f"qhat{i}", [64, TB], BF16) for i in range(2)]
        ptile = [sbuf(em, f"pt{i}", [128, TB], BF16) for i in range(4)]
        rcpb = sbuf(em, "rcpb", [4, TB], BF16)
        rcpf = sbuf(em, "rcpf", [1, TB], F32)
        rbsb = sbuf(em, "rbsb", [64, TB], F32)
        em.op("gpsimd", lambda g: g.memset(rcpb[:], 0.0), s=P.sg)
        rcpb_z = P.sg.at()
        if layer == 1:
            cmr = sbuf(em, "cmr", [128, NT * 256], BF16)
            scm = em.sem("scm")
            cmd = C["cmask_d"]
            em.op("gpsimd", lambda g, cmd=cmd: g.dma_start(out=cmr[:], in_=cmd), s=scm, k=16)
            cm_w = scm.at()

        py_ev = [None, None]
        pt_used = [None] * 4
        qhat_used = [None, None]
        attnT_w = {}
        banks4 = [P.PB[:, 0:512], P.PB[:, 512:1024], P.PA[:, 0:512], P.PA[:, 512:1024]]
        pb_ev4 = [pb_ev[0], pb_ev[1], pa_ev4q[0], pa_ev4q[1]]
        exp_ev = [None] * 4
        for h in range(NH):
            hb = h % 2
            wv_ = [cc_done]
            if kh_used[hb] is not None:
                wv_.append(kh_used[hb])
            for r in range(2):
                em.op("sync", lambda y, r=r, h=h, hb=hb: y.dma_start(
                    out=kh[hb][0:64, r * T:(r + 1) * T],
                    in_=kcoll_out[r * D + h * 64: r * D + h * 64 + 64, :]),
                    w=wv_ if r == 0 else (), s=skh[hb], k=16)
            khw = skh[hb].at()
            qrow = h // 2
            qoff = (h % 2) * 64
            for qb in range(QB):
                qhb = qb % 2
                wq_ = [qhat_used[qhb]] if qhat_used[qhb] is not None else [cc_done]
                em.op("vector", lambda v, qhb=qhb, qrow=qrow, qoff=qoff, qb=qb: v.tensor_copy(
                    qhat[qhb][0:64, :],
                    rotQ[qoff:qoff + 64, qrow * T + qb * TB: qrow * T + (qb + 1) * TB]),
                    w=wq_, s=P.sv)
                qc_w = P.sv.at()
                gmax_t = 2 * (qb * 4 + 3) + 2 if layer == 1 else KT
                pyh = P.PY[:, qhb * 512:(qhb + 1) * 512]

                def emit_passB(g, first, qb=qb, qc_w=qc_w, khw=khw, hb=hb, qhb=qhb):
                    bank = banks4[g % 4]
                    qs = max(0, (g // 2 - qb * 4)) * 128 if layer == 1 else 0
                    n = TB - qs
                    wv_ = [pb_ev4[g % 4]]
                    if first:
                        wv_.append(qc_w)
                    if qb == 0 and g == 0:
                        wv_.append(khw)
                    kc = (g % 2) * T + (g // 2) * 128
                    em.op("tensor", lambda t, bank=bank, qs=qs, n=n, hb=hb, qhb=qhb, kc=kc: t.matmul(
                        bank[:, qs:qs + n], kh[hb][0:64, kc:kc + 128],
                        qhat[qhb][0:64, qs:qs + n], start=True, stop=True),
                        w=wv_, s=P.st)
                    mmw = P.st.at()
                    exw = [mmw]
                    if layer == 1:
                        ti_loc = g // 2 - qb * 4
                        if 0 <= ti_loc < 4:
                            em.op("vector", lambda v, bank=bank, g=g, ti_loc=ti_loc: v.tensor_tensor(
                                bank[:, ti_loc * 128:(ti_loc + 1) * 128],
                                bank[:, ti_loc * 128:(ti_loc + 1) * 128],
                                cmr[:, (g // 2) * 256 + (g % 2) * 128:
                                    (g // 2) * 256 + (g % 2) * 128 + 128],
                                ALU.add), w=[mmw, cm_w], s=P.sv)
                            exw = [P.sv.at()]
                    pslot = g % 4
                    if pt_used[pslot] is not None:
                        exw.append(pt_used[pslot])
                    em.op("scalar", lambda sc, bank=bank, qs=qs, n=n, pslot=pslot: sc.activation(
                        ptile[pslot][:, qs:qs + n], bank[:, qs:qs + n], AF.Exp,
                        scale=scale), w=exw, s=P.ss)
                    pb_ev4[pslot] = P.ss.at()
                    exp_ev[pslot] = P.ss.at()

                def emit_AV(g, qb=qb, pyh=pyh, h=h, qhb=qhb, gmax_t=gmax_t):
                    qs = max(0, (g // 2 - qb * 4)) * 128 if layer == 1 else 0
                    n = TB - qs
                    pslot = g % 4
                    wv2 = [exp_ev[pslot]]
                    if g == 0:
                        wv2.append(vhat_w)
                        if py_ev[qhb] is not None:
                            wv2.append(py_ev[qhb])
                    em.op("tensor", lambda t, g=g, pyh=pyh, qs=qs, n=n, h=h, pslot=pslot, gmax_t=gmax_t: t.matmul(
                        pyh[0:65, qs:qs + n], vhat(g)[:, h * 65:(h + 1) * 65],
                        ptile[pslot][:, qs:qs + n],
                        start=(g == 0), stop=(g == gmax_t - 1)), w=wv2, s=P.st)
                    pt_used[pslot] = P.st.at()

                npro = min(4, gmax_t)
                for g in range(npro):
                    emit_passB(g, g == 0)
                for g in range(gmax_t):
                    emit_AV(g)
                    if g + 4 < gmax_t:
                        emit_passB(g + 4, False)
                av_w = P.st.at()
                qhat_used[qhb] = av_w
                em.op("vector", lambda v, pyh=pyh: v.reciprocal(rcpf[:], pyh[64:65, :]),
                      w=[av_w, rcpb_z] if (h == 0 and qb == 0) else [av_w], s=P.sv)
                rc1 = P.sv.at()
                vfence(em, P, rcpf[:])
                em.op("vector", lambda v: v.tensor_copy(rcpb[0:1, :], rcpf[:]),
                      w=[rc1], s=P.sv)
                rw_ = P.sv.at()
                em.op("tensor", lambda t: t.matmul(
                    P.PM[0:64, 0:512], bc4[0:4, 0:64], rcpb[0:4, :], start=True, stop=True),
                    w=[rw_, (P.ss, P.ss.n)], s=P.st)
                tw = P.st.at()
                em.op("scalar", lambda sc: sc.copy(rbsb[:], P.PM[0:64, 0:512]), w=[tw], s=P.ss)
                rbw = P.ss.at()
                em.op("vector", lambda v, pyh=pyh, h=h, qb=qb: v.tensor_tensor(
                    attnT(qb, h // 2)[(h % 2) * 64:(h % 2) * 64 + 64, :],
                    pyh[0:64, :], rbsb[:], ALU.mult), w=[rbw], s=P.sv)
                py_ev[qhb] = P.sv.at()
                attnT_w[(qb, h)] = P.sv.at()
            kh_used[hb] = P.st.at()
        scores_done = P.sv.at()

        if layer == 0:
            wout_sb = sbuf(em, "wout_sb", [128, DC * 1024], BF16)
            swo = em.sem("swo")
            wo_d = W["W_out"]
            em.op("gpsimd", lambda g, wo_d=wo_d: g.dma_start(
                out=wout_sb[:].rearrange("k (c f) -> k c f", c=DC),
                in_=wo_d.rearrange("(c k) f -> k c f", c=DC)), s=swo, k=16)
            wo_w = swo.at()

            def woslice(fc, cc):
                return wout_sb[:, fc * 1024 + cc * 128: fc * 1024 + (cc + 1) * 128]
        else:
            woT = sbuf(em, "woT", [128, 64 * 128], BF16)
            swo = em.sem("swo")
            wo_d = W["wo"]
            wo_used = [None, None]
            tdone2 = [None, None]
            for fc in range(DC):
                b = fc % 2
                wld = [wo_used[b]] if wo_used[b] is not None else [(P.ss, P.ss.n)]
                em.op("gpsimd", lambda g, fc=fc, b=b, wo_d=wo_d: g.dma_start(
                    out=raw[b][:, 0:1024], in_=wo_d[fc * 128:(fc + 1) * 128, :]), w=wld, s=swo, k=16)
                dw = swo.at()
                for dc in range(DC):
                    pm = P.PM[:, 128:256] if dc % 2 == 0 else P.P8[:, 128:256]
                    wv_ = [dw] if dc == 0 else []
                    if tdone2[dc % 2] is not None:
                        wv_.append(tdone2[dc % 2])
                    wv_.append((P.ss, P.ss.n))
                    em.op("tensor", lambda t, b=b, dc=dc, pm=pm: t.matmul(
                        pm, raw[b][:, dc * 128:(dc + 1) * 128], idn_bf[:],
                        start=True, stop=True), w=wv_, s=P.st)
                    tw = P.st.at()
                    em.op("scalar", lambda sc, dc=dc, fc=fc, pm=pm: sc.copy(
                        woT[:, (dc * DC + fc) * 128:(dc * DC + fc) * 128 + 128], pm),
                        w=[tw], s=P.ss)
                    tdone2[dc % 2] = P.ss.at()
                wo_used[b] = P.ss.at()
            wo_w = P.ss.at()

            def woslice(fc, cc):
                return woT[:, (fc * DC + cc) * 128:(fc * DC + cc) * 128 + 128]

        xrow = [sbuf(em, f"xrow{i}", [128, TB], F32) for i in range(2)]
        sxl = [em.sem("sxl0"), em.sem("sxl1")]
        xrow_used = [None, None]
        pp_ev = [None, None]
        for qb in range(QB):
            for cc in range(DC):
                b = (qb * DC + cc) % 2
                half = b
                pp = P.PM[:, 0:512] if half == 0 else P.P8[:, 0:512]
                wv_ = [xrow_used[b]] if xrow_used[b] is not None else list(stash_w)
                em.op("sync", lambda y, qb=qb, cc=cc, b=b: y.dma_start(
                    out=xrow[b][:], in_=xsrc[cc * 128:(cc + 1) * 128, qb * TB:(qb + 1) * TB]),
                    w=wv_, s=sxl[b], k=16)
                xw = sxl[b].at()
                wv2 = [attnT_w[(qb, NH - 1)], wo_w, scores_done]
                if pp_ev[half] is not None:
                    wv2.append(pp_ev[half])
                for fc in range(DC):
                    em.op("tensor", lambda t, fc=fc, cc=cc, qb=qb, pp=pp: t.matmul(
                        pp, woslice(fc, cc), attnT(qb, fc),
                        start=(fc == 0), stop=(fc == DC - 1)),
                        w=wv2 if fc == 0 else (), s=(P.st if fc == DC - 1 else None))
                mmw = P.st.at()
                em.op("vector", lambda v, cc=cc, qb=qb, b=b, pp=pp: v.tensor_tensor(
                    P.XT[cc][:, qb * TB:(qb + 1) * TB], xrow[b][:], pp, ALU.add),
                    w=[mmw, xw], s=P.sv)
                pp_ev[half] = P.sv.at()
                xrow_used[b] = P.sv.at()
        P.dbg_refs = {"rotQ": rotQ, "kh0": kh[0], "kh1": kh[1]}
        if os.environ.get("KDBG", "") == "1":
            sdbg = em.sem("sdbg")
            wd = [(P.sv, P.sv.n), (P.ss, P.ss.n), (P.st, P.st.n)]
            nc2 = em.nc
            d_attnT = nc2.dram_tensor("dbg_attnT", [128, DC * T], BF16, kind="ExternalOutput").ap()
            d_rotq = nc2.dram_tensor("dbg_rotq", [128, DC * T], BF16, kind="ExternalOutput").ap()
            d_kh0 = nc2.dram_tensor("dbg_kh0", [64, S], BF16, kind="ExternalOutput").ap()
            d_kcoll = nc2.dram_tensor("dbg_kcoll", [2 * D, T], BF16, kind="ExternalOutput").ap()
            d_vcoll = nc2.dram_tensor("dbg_vcoll", [2 * T, NH * 65], BF16, kind="ExternalOutput").ap()
            em.op("sync", lambda y: y.dma_start(out=d_attnT, in_=P.XHBIG[:]), w=wd, s=sdbg, k=16)
            em.op("sync", lambda y: y.dma_start(out=d_rotq, in_=rotQ[:]), s=sdbg, k=16)
            em.op("sync", lambda y: y.dma_start(out=d_kh0, in_=kh[0][:]), s=sdbg, k=16)
            em.op("sync", lambda y: y.dma_start(out=d_kcoll, in_=kcoll_out), s=sdbg, k=16)
            em.op("sync", lambda y: y.dma_start(out=d_vcoll, in_=vcoll_out), s=sdbg, k=16)
            d_vcin = nc2.dram_tensor("dbg_vcin", [T, NH * 65], BF16, kind="ExternalOutput").ap()
            em.op("sync", lambda y: y.dma_start(out=d_vcin, in_=vcoll_in), s=sdbg, k=16)
            em.barrier_wait("sync", [sdbg.at()])
        em.stack = outer
    P.attn_done = P.sv.at()


def const_inputs(nc):
    C = {}
    C["idn"] = nc.dram_tensor("c_idn", [128, 128], F32, kind="ExternalInput").ap()
    C["bc4_d"] = nc.dram_tensor("c_bc4", [4, 128], F32, kind="ExternalInput").ap()
    C["esel0_d"] = nc.dram_tensor("c_esel0", [E0, E0 * 128], F32, kind="ExternalInput").ap()
    C["esel1_d"] = nc.dram_tensor("c_esel1", [E1, E1 * 128], F32, kind="ExternalInput").ap()
    return C


def load_consts(em, P, C, need=("idn",)):
    scst = em.sem("scst")
    scsg = em.sem("scsg")
    ones_bc = sbuf(em, "c_ones_bc", [1, 128], BF16)
    ones_sum = sbuf(em, "c_ones_sum", [128, 1], BF16)
    em.op("gpsimd", lambda g: g.memset(ones_bc[:], 1.0), s=P.sg)
    em.op("gpsimd", lambda g: g.memset(ones_sum[:], 1.0), s=P.sg)
    bc4 = sbuf(em, "c_bc4_sb", [4, 128], BF16)
    bc4_d = C["bc4_d"]
    em.op("gpsimd", lambda g, bc4_d=bc4_d: g.dma_start(out=bc4[:], in_=bc4_d), s=scsg, k=16)
    C["bc4"] = bc4
    eps = sbuf(em, "c_eps", [1, 1], F32)
    em.op("gpsimd", lambda g: g.memset(eps[:], 1e-5), s=P.sg)
    C["ones_bc"] = ones_bc
    C["ones_sum"] = ones_sum
    C["eps"] = eps
    if "idn" in need:
        idn = sbuf(em, "c_idn_sb", [128, 128], F32)
        idn_d = C["idn"]
        em.op("sync", lambda y, idn_d=idn_d: y.dma_start(out=idn[:], in_=idn_d), s=scst, k=16)
        idnb = sbuf(em, "c_idnb_sb", [128, 128], BF16)
        em.op("gpsimd", lambda g, idn_d=idn_d: g.dma_start(out=idnb[:], in_=idn_d), s=scsg, k=16)
        C["idn"] = idn
        C["idn_bf"] = idnb
    em.barrier_wait("vector", [scst.at(), scsg.at(), (P.sg, P.sg.n)])
    em.barrier_wait("scalar", [scst.at(), scsg.at(), (P.sg, P.sg.n)])
    em.barrier_wait("tensor", [scst.at(), scsg.at(), (P.sg, P.sg.n)])


def build_phase(phase):
    nc = bass.Bass("TRN2", target_bir_lowering=False)
    stack = contextlib.ExitStack()
    em = Em(nc, stack)
    with stack:
        C = const_inputs(nc)
        if phase == "p_in":
            x_tok = nc.dram_tensor("x_tok", [T, D], F32, kind="ExternalInput").ap()
            xt_out = nc.dram_tensor("xt_out", [D, T], F32, kind="ExternalOutput").ap()
            P = make_ctx(em)
            load_consts(em, P, C, need=("idn",))
            emit_tin(em, P, x_tok, C)
            emit_store_xt(em, P, xt_out)
        elif phase == "p_out":
            xt_in = nc.dram_tensor("xt_in", [D, T], F32, kind="ExternalInput").ap()
            x_out = nc.dram_tensor("x_out", [T, D], F32, kind="ExternalOutput").ap()
            P = make_ctx(em)
            load_consts(em, P, C, need=("idn",))
            emit_load_xt(em, P, xt_in)
            emit_tout(em, P, x_out, C)
        elif phase in ("ffn0", "ffn1"):
            layer = 0 if phase == "ffn0" else 1
            xt_in = nc.dram_tensor("xt_in", [D, T], F32, kind="ExternalInput").ap()
            xt_out = nc.dram_tensor("xt_out", [D, T], F32, kind="ExternalOutput").ap()
            W = {}
            if layer == 0:
                W["gate_w"] = nc.dram_tensor("L0_ffn_gate_w", [E0, D, H0], F32, kind="ExternalInput").ap()
                W["gate_b"] = nc.dram_tensor("L0_ffn_gate_b", [E0, H0], F32, kind="ExternalInput").ap()
                W["up_w"] = nc.dram_tensor("L0_ffn_up_w", [E0, D, H0], F32, kind="ExternalInput").ap()
                W["up_b"] = nc.dram_tensor("L0_ffn_up_b", [E0, H0], F32, kind="ExternalInput").ap()
                W["down_w"] = nc.dram_tensor("L0_ffn_down_w", [E0, H0, D], F32, kind="ExternalInput").ap()
                W["down_b"] = nc.dram_tensor("L0_ffn_down_b", [E0, D], F32, kind="ExternalInput").ap()
                W["router_w"] = nc.dram_tensor("L0_ffn_router_w", [D, E0], F32, kind="ExternalInput").ap()
            else:
                W["w1"] = nc.dram_tensor("L1_ffn_w1", [E1, D, F1], F32, kind="ExternalInput").ap()
                W["b1"] = nc.dram_tensor("L1_ffn_b1", [E1, F1], F32, kind="ExternalInput").ap()
                W["w2"] = nc.dram_tensor("L1_ffn_w2", [E1, F1, D], F32, kind="ExternalInput").ap()
                W["b2"] = nc.dram_tensor("L1_ffn_b2", [E1, D], F32, kind="ExternalInput").ap()
                W["router_w"] = nc.dram_tensor("L1_ffn_router_w", [D, E1], F32, kind="ExternalInput").ap()
            P = make_ctx(em)
            load_consts(em, P, C, need=("idn",))
            emit_load_xt(em, P, xt_in)
            P.dbg_out = xt_out
            emit_ffn(em, P, C, W, layer)
            if not getattr(P, "dbg_done", False):
                emit_store_xt(em, P, xt_out)
        elif phase in ("attn0", "attn1"):
            layer = 0 if phase == "attn0" else 1
            xt_in = nc.dram_tensor("xt_in", [D, T], F32, kind="ExternalInput").ap()
            xt_out = nc.dram_tensor("xt_out", [D, T], F32, kind="ExternalOutput").ap()
            W = {}
            if layer == 0:
                W["W_qkv"] = nc.dram_tensor("L0_attn_W_qkv", [D, 3 * D], F32, kind="ExternalInput").ap()
                W["W_out"] = nc.dram_tensor("L0_attn_W_out", [D, D], F32, kind="ExternalInput").ap()
            else:
                for nm in ("wq", "wk", "wv", "wo"):
                    W[nm] = nc.dram_tensor(f"L1_attn_{nm}", [D, D], F32, kind="ExternalInput").ap()
                C["cos_d"] = nc.dram_tensor("c_cos", [32, T], F32, kind="ExternalInput").ap()
                C["sin_d"] = nc.dram_tensor("c_sin", [32, T], F32, kind="ExternalInput").ap()
                C["cmask_d"] = nc.dram_tensor("c_cmask", [128, NT * 256], F32, kind="ExternalInput").ap()
            P = make_ctx(em)
            load_consts(em, P, C, need=("idn",))
            emit_load_xt(em, P, xt_in)
            emit_attn(em, P, C, W, layer, xt_in)
            emit_store_xt(em, P, xt_out)
        else:
            raise ValueError(phase)
        em.flush(phase)
    return nc


def build_fused():
    nc = bass.Bass("TRN2", target_bir_lowering=False)
    stack = contextlib.ExitStack()
    em = Em(nc, stack)
    with stack:
        C = const_inputs(nc)
        C["cos_d"] = nc.dram_tensor("c_cos", [32, T], F32, kind="ExternalInput").ap()
        C["sin_d"] = nc.dram_tensor("c_sin", [32, T], F32, kind="ExternalInput").ap()
        C["cmask_d"] = nc.dram_tensor("c_cmask", [128, NT * 256], F32, kind="ExternalInput").ap()
        xt_in = nc.dram_tensor("xt_in", [D, T], F32, kind="ExternalInput").ap()
        xt_out = nc.dram_tensor("xt_out", [D, T], F32, kind="ExternalOutput").ap()
        W0f = {
            "gate_w": nc.dram_tensor("L0_ffn_gate_w", [E0, D, H0], F32, kind="ExternalInput").ap(),
            "gate_b": nc.dram_tensor("L0_ffn_gate_b", [E0, H0], F32, kind="ExternalInput").ap(),
            "up_w": nc.dram_tensor("L0_ffn_up_w", [E0, D, H0], F32, kind="ExternalInput").ap(),
            "up_b": nc.dram_tensor("L0_ffn_up_b", [E0, H0], F32, kind="ExternalInput").ap(),
            "down_w": nc.dram_tensor("L0_ffn_down_w", [E0, H0, D], F32, kind="ExternalInput").ap(),
            "down_b": nc.dram_tensor("L0_ffn_down_b", [E0, D], F32, kind="ExternalInput").ap(),
            "router_w": nc.dram_tensor("L0_ffn_router_w", [D, E0], F32, kind="ExternalInput").ap(),
        }
        W1f = {
            "w1": nc.dram_tensor("L1_ffn_w1", [E1, D, F1], F32, kind="ExternalInput").ap(),
            "b1": nc.dram_tensor("L1_ffn_b1", [E1, F1], F32, kind="ExternalInput").ap(),
            "w2": nc.dram_tensor("L1_ffn_w2", [E1, F1, D], F32, kind="ExternalInput").ap(),
            "b2": nc.dram_tensor("L1_ffn_b2", [E1, D], F32, kind="ExternalInput").ap(),
            "router_w": nc.dram_tensor("L1_ffn_router_w", [D, E1], F32, kind="ExternalInput").ap(),
        }
        W0a = {
            "W_qkv": nc.dram_tensor("L0_attn_W_qkv", [D, 3 * D], F32, kind="ExternalInput").ap(),
            "W_out": nc.dram_tensor("L0_attn_W_out", [D, D], F32, kind="ExternalInput").ap(),
        }
        W1a = {nm: nc.dram_tensor(f"L1_attn_{nm}", [D, D], F32, kind="ExternalInput").ap()
               for nm in ("wq", "wk", "wv", "wo")}
        P = make_ctx(em)
        P.dbg_out = None
        load_consts(em, P, C, need=("idn",))
        emit_load_xt(em, P, xt_in)
        P.xstash = nc.dram_tensor("xstash", [D, T], F32).ap()
        phases = [("ffn", 0, W0f), ("attn", 0, W0a), ("ffn", 0, W0f),
                  ("ffn", 1, W1f), ("attn", 1, W1a), ("ffn", 1, W1f)]
        for i, (knd, layer, Wm) in enumerate(phases):
            em.prefix = f"p{i}_"
            if i > 0:
                em.phase_barrier([P.st.at(), P.ss.at(), P.sv.at(), P.sg.at()])
                P.xt_ready = P.sv.at()
            if knd == "ffn":
                emit_ffn(em, P, C, Wm, layer)
            else:
                sstash = em.sem("sstash")
                for c in range(DC):
                    em.op("sync", lambda y, c=c: y.dma_start(
                        out=P.xstash[c * 128:(c + 1) * 128, :], in_=P.XT[c]),
                        s=sstash, k=16)
                P.stash_done = sstash.at()
                emit_attn(em, P, C, Wm, layer, P.xstash)
        em.prefix = ""
        em.phase_barrier([P.st.at(), P.ss.at(), P.sv.at(), P.sg.at()])
        emit_store_xt(em, P, xt_out)
        em.flush("fused")
    return nc


_NC_CACHE = {}


def get_phase(phase):
    if phase not in _NC_CACHE:
        _NC_CACHE[phase] = build_phase(phase)
    return _NC_CACHE[phase]


def host_consts():
    C = {}
    C["c_idn"] = np.eye(128, dtype=np.float32)
    bc4 = np.zeros((4, 128), dtype=np.float32)
    bc4[0, :] = 1.0
    C["c_bc4"] = bc4
    e0 = np.zeros((E0, E0 * 128), dtype=np.float32)
    for e in range(E0):
        e0[e, e * 128:(e + 1) * 128] = 1.0
    C["c_esel0"] = e0
    e1 = np.zeros((E1, E1 * 128), dtype=np.float32)
    for e in range(E1):
        e1[e, e * 128:(e + 1) * 128] = 1.0
    C["c_esel1"] = e1
    return C


def host_rope(par):
    t = np.arange(T)
    pos = (t // 128) * 256 + par * 128 + (t % 128)
    powers = np.arange(0, DKH, 2, dtype=np.float64) / DKH
    freqs = 1.0 / (10000.0 ** powers)
    ang = np.outer(freqs, pos.astype(np.float64))
    return np.cos(ang).astype(np.float32), np.sin(ang).astype(np.float32)


def host_cmask(par):
    m = np.zeros((128, NT * 256), dtype=np.float32)
    kk = np.arange(128)[:, None]
    qq = np.arange(128)[None, :]
    tri = np.where(kk > qq, NEG, 0.0).astype(np.float32)
    for ti in range(NT):
        if par == 0:
            m[:, ti * 256:ti * 256 + 128] = tri
            m[:, ti * 256 + 128:ti * 256 + 256] = NEG
        else:
            m[:, ti * 256:ti * 256 + 128] = 0.0
            m[:, ti * 256 + 128:ti * 256 + 256] = tri
    return m


def shard_rows(x):
    out = []
    for c in range(8):
        b, p = c // 2, c % 2
        xb = np.ascontiguousarray(x[b].reshape(32, 128, D)[p::2].reshape(T, D))
        out.append(xb)
    return out


def unshard_rows(parts):
    x = np.zeros((4, S, D), dtype=np.float32)
    for c in range(8):
        b, p = c // 2, c % 2
        xr = x[b].reshape(32, 128, D)
        xr[p::2] = np.asarray(parts[c], dtype=np.float32).reshape(NT, 128, D)
    return x


def run_phase(phase, per_core_inputs):
    nc = get_phase(phase)
    res = run_bass_kernel_spmd(nc, per_core_inputs, core_ids=list(range(8)))
    return res.results


WEIGHT_NAMES = (
    "L0_attn_W_qkv", "L0_attn_W_out",
    "L0_ffn_gate_w", "L0_ffn_gate_b", "L0_ffn_up_w", "L0_ffn_up_b",
    "L0_ffn_down_w", "L0_ffn_down_b", "L0_ffn_router_w",
    "L1_attn_wq", "L1_attn_wk", "L1_attn_wv", "L1_attn_wo",
    "L1_ffn_router_w", "L1_ffn_w1", "L1_ffn_b1", "L1_ffn_w2", "L1_ffn_b2")


def kernel(**inputs):
    if os.environ.get("KRUN", "fused") == "fused":
        return kernel_fused(**inputs)
    return kernel_phases(**inputs)


def kernel_fused(**inputs):
    hc = host_consts()
    xs = shard_rows(np.asarray(inputs["L0_attn_X"], dtype=np.float32))
    xts = [np.ascontiguousarray(xs[c].T) for c in range(8)]
    wts = {k: np.asarray(inputs[k], dtype=np.float32) for k in WEIGHT_NAMES}
    if "fused" not in _NC_CACHE:
        _NC_CACHE["fused"] = build_fused()
    nc = _NC_CACHE["fused"]
    pcs = []
    for c in range(8):
        par = c % 2
        cosr, sinr = host_rope(par)
        d = dict(hc, **wts)
        d["c_cos"] = cosr
        d["c_sin"] = sinr
        d["c_cmask"] = host_cmask(par)
        d["xt_in"] = xts[c]
        pcs.append(d)
    res = run_bass_kernel_spmd(nc, pcs, core_ids=list(range(8)))
    parts = [np.asarray(res.results[c]["xt_out"], dtype=np.float32).T for c in range(8)]
    return unshard_rows(parts)


def kernel_phases(**inputs):
    hc = host_consts()
    xs = shard_rows(np.asarray(inputs["L0_attn_X"], dtype=np.float32))
    xts = [np.ascontiguousarray(xs[c].T) for c in range(8)]

    pc = []
    for c in range(8):
        par = c % 2
        cosr, sinr = host_rope(par)
        pc.append(dict(hc, c_cos=cosr, c_sin=sinr, c_cmask=host_cmask(par)))

    w_ffn0 = {k: np.asarray(inputs[k], dtype=np.float32) for k in (
        "L0_ffn_gate_w", "L0_ffn_gate_b", "L0_ffn_up_w", "L0_ffn_up_b",
        "L0_ffn_down_w", "L0_ffn_down_b", "L0_ffn_router_w")}
    w_ffn1 = {k: np.asarray(inputs[k], dtype=np.float32) for k in (
        "L1_ffn_w1", "L1_ffn_b1", "L1_ffn_w2", "L1_ffn_b2", "L1_ffn_router_w")}
    w_attn0 = {k: np.asarray(inputs[k], dtype=np.float32) for k in (
        "L0_attn_W_qkv", "L0_attn_W_out")}
    w_attn1 = {k: np.asarray(inputs[k], dtype=np.float32) for k in (
        "L1_attn_wq", "L1_attn_wk", "L1_attn_wv", "L1_attn_wo")}

    def strip(d, names):
        return {k: d[k] for k in names}

    seq = [
        ("ffn0", w_ffn0), ("attn0", w_attn0), ("ffn0", w_ffn0),
        ("ffn1", w_ffn1), ("attn1", w_attn1), ("ffn1", w_ffn1),
    ]
    cur = xts
    for phase, wmap in seq:
        need = {"ffn0": ("c_idn", "c_bc4", "c_esel0", "c_esel1"),
                "ffn1": ("c_idn", "c_bc4", "c_esel0", "c_esel1"),
                "attn0": ("c_idn", "c_bc4", "c_esel0", "c_esel1"),
                "attn1": ("c_idn", "c_bc4", "c_esel0", "c_esel1",
                          "c_cos", "c_sin", "c_cmask")}[phase]
        res = run_phase(phase, [dict(strip(pc[c], need), xt_in=cur[c], **wmap) for c in range(8)])
        cur = [r["xt_out"] for r in res]
    parts = [np.asarray(cur[c], dtype=np.float32).T for c in range(8)]
    return unshard_rows(parts)


if __name__ == "__main__":
    pass


# version 10
# speedup vs baseline: 1.0347x; 1.0347x over previous
import contextlib
import os
import numpy as np
import ml_dtypes
import concourse.bass as bass
import concourse.mybir as mybir
from concourse.bass_utils import run_bass_kernel_spmd

F32 = mybir.dt.float32
BF16 = mybir.dt.bfloat16
AF = mybir.ActivationFunctionType
ALU = mybir.AluOpType
AX = mybir.AxisListType

D, DC, T, NT, TB, NTB = 1024, 8, 2048, 16, 512, 4
S, NH, DKH, KT = 4096, 16, 64, 32
E0, H0 = 4, 2730
HC0 = [128] * 21 + [42]
SLAB0 = [(0, 6), (6, 12), (12, 18), (18, 22)]
E1, F1 = 14, 2048
SLAB1 = [(0, 8), (8, 16)]
NEG = -1.0e30
BF = ml_dtypes.bfloat16


class Sem:
    def __init__(self, h):
        self.h = h
        self.n = 0

    def bump(self, k=1):
        self.n += k
        return self.n

    def at(self):
        return (self, self.n)


class Em:
    ENGS = ("tensor", "scalar", "vector", "gpsimd", "sync")

    def __init__(self, nc, stack):
        self.nc = nc
        self.stack = stack
        self.sem_stack = stack
        self.sems = {}
        self.prefix = ""
        self.phase_waits = None
        self.phase_claimed = set()
        self.ops = {e: [] for e in self.ENGS}
        self._si = 0
        self.paranoid = os.environ.get("KSYNC", "") == "paranoid"
        self.psems = {}

    def sem(self, name=None):
        if name is not None and name in self.sems:
            return self.sems[name]
        h = self.sem_stack.enter_context(self.nc.semaphore(name or f"s{self._si}"))
        self._si += 1
        sm = Sem(h)
        if name is not None:
            self.sems[name] = sm
        return sm

    def phase_barrier(self, waits):
        self.phase_waits = list(waits)
        self.phase_claimed = set()

    def op(self, eng, fn, w=(), s=None, k=1):
        if self.phase_waits is not None and eng not in self.phase_claimed:
            w = list(self.phase_waits) + list(w)
            self.phase_claimed.add(eng)
        paranoid = self.paranoid and eng in ("tensor", "scalar", "vector")
        if paranoid:
            w = list(w) + [o.at() for e2, o in self.psems.items() if e2 != eng]
        if s is not None:
            s.bump(k)
        w = tuple((sem.h, val) for sem, val in w)
        if paranoid:
            ps = self.psems[eng]
            ps.bump(1)

            def body(E, fn=fn, w=w, s=s, k=k, ps=ps):
                for h, val in w:
                    E.wait_ge(h, val)
                inst = fn(E)
                if s is not None:
                    inst.then_inc(s.h, k)
                E.nop().then_inc(ps.h, 1)
            self.ops[eng].append(body)
            return

        def body(E, fn=fn, w=w, s=s, k=k):
            for h, val in w:
                E.wait_ge(h, val)
            inst = fn(E)
            if s is not None:
                inst.then_inc(s.h, k)
        self.ops[eng].append(body)

    def barrier_wait(self, eng, w):
        self.op(eng, lambda E: E.nop(), w=w)

    def flush(self, name=None):
        nc = self.nc
        todo = {e: lst for e, lst in self.ops.items() if lst}
        self.ops = {e: [] for e in self.ENGS}
        with nc.Block(name) as blk:
            for e, lst in todo.items():
                def run(E, lst=lst):
                    for f in lst:
                        f(E)
                getattr(blk, e)(run)


def sfence(em, P, src, s_=None, k=1, w=()):
    p, f = src.shape[-2], src.shape[-1]
    f = min(f, 256)
    em.op("scalar", lambda sc, src=src, p=p, f=f: sc.copy(P.junk[0:p, 256:256 + f], src[0:p, 0:f]),
          w=w, s=s_, k=k)


def vfence(em, P, src, s_=None, k=1):
    p, f = src.shape[-2], src.shape[-1]
    f = min(f, 256)
    em.op("vector", lambda v, src=src, p=p, f=f: v.tensor_copy(P.junk[0:p, 0:f], src[0:p, 0:f]), s=s_, k=k)


def sbuf(em, name, shape, dt):
    return em.stack.enter_context(em.nc.sbuf_tensor(em.prefix + name, shape, dt))


def psum(em, name, shape, dt=F32):
    return em.stack.enter_context(em.nc.psum_tensor(name, shape, dt))


class Ctx:
    pass


def make_ctx(em):
    P = Ctx()
    if em.paranoid:
        em.psems = {e: em.sem(f"par_{e}") for e in ("tensor", "scalar", "vector")}
    P.XTBIG = sbuf(em, "XTBIG", [128, DC * T], F32)
    P.XHBIG = sbuf(em, "XHBIG", [128, DC * T], BF16)
    P.XT = [P.XTBIG[:, c * T:(c + 1) * T] for c in range(DC)]
    P.XH = [P.XHBIG[:, c * T:(c + 1) * T] for c in range(DC)]
    P.PA = psum(em, "PA", [128, 1024])
    P.PB = psum(em, "PB", [128, 1024])
    P.PY = psum(em, "PY", [128, 1024])
    P.PM = psum(em, "PM", [128, 512])
    P.P8 = psum(em, "P8", [128, 512])
    P.junk = sbuf(em, "junk", [128, 512], F32)
    P.st = em.sem("st")
    P.ss = em.sem("ss")
    P.sv = em.sem("sv")
    P.sg = em.sem("sg")
    P.sgd = em.sem("sgd")
    P.syd = em.sem("syd")
    P.xh_ready = None
    return P


def emit_load_xt(em, P, xt_dram):
    sld = em.sem("xtld")
    for c in range(DC):
        em.op("sync", lambda y, c=c: y.dma_start(
            out=P.XT[c], in_=xt_dram[c * 128:(c + 1) * 128, :]), s=sld, k=16)
    P.xt_ready = sld.at()


def emit_store_xt(em, P, xt_dram):
    sst = em.sem("xtst")
    w = [(P.sv, P.sv.n), (P.ss, P.ss.n)]
    for c in range(DC):
        em.op("sync", lambda y, c=c: y.dma_start(
            out=xt_dram[c * 128:(c + 1) * 128, :], in_=P.XT[c]),
            w=w if c == 0 else (), s=sst, k=16)
    em.barrier_wait("sync", [sst.at()])


def emit_tin(em, P, x_tok, C):
    idn = C["idn"]
    ring = [sbuf(em, f"tin{i}", [128, D], F32) for i in range(2)]
    sld = [em.sem("tinl0"), em.sem("tinl1")]
    evac_done = [(P.sv, 0), (P.sv, 0)]
    slots = [P.PA[:, 0:128], P.PA[:, 512:640], P.PB[:, 0:128], P.PB[:, 512:640]]
    slot_evac = [None, None, None, None]
    for ti in range(NT):
        b = ti % 2
        em.op("sync", lambda y, ti=ti, b=b: y.dma_start(
            out=ring[b][:], in_=x_tok[ti * 128:(ti + 1) * 128, :]),
            w=[evac_done[b]] if ti >= 2 else (), s=sld[b], k=16)
        dmaw = sld[b].at()
        for c in range(DC):
            sl = slots[c % 4]
            w2 = [dmaw] if c == 0 else []
            if slot_evac[c % 4] is not None:
                w2.append(slot_evac[c % 4])
            em.op("tensor", lambda t, c=c, b=b, sl=sl: t.transpose(
                sl, ring[b][:, c * 128:(c + 1) * 128], idn[:]),
                w=w2, s=P.st)
            tw = P.st.at()
            em.op("vector", lambda v, c=c, ti=ti, sl=sl: v.tensor_copy(
                P.XT[c][:, ti * 128:(ti + 1) * 128], sl),
                w=[tw], s=P.sv)
            slot_evac[c % 4] = P.sv.at()
        evac_done[b] = P.sv.at()
    P.xt_ready = P.sv.at()


def emit_tout(em, P, x_out, C):
    idn = C["idn"]
    ring = [sbuf(em, f"tout{i}", [128, D], F32) for i in range(2)]
    sst = [em.sem("touts0"), em.sem("touts1")]
    store_done = [(sst[0], 0), (sst[1], 0)]
    w0 = [(P.sv, P.sv.n), (P.ss, P.ss.n)]
    if getattr(P, "xt_ready", None) is not None:
        w0.append(P.xt_ready)
    slots = [P.PA[:, 0:128], P.PA[:, 512:640], P.PB[:, 0:128], P.PB[:, 512:640]]
    slot_evac = [None, None, None, None]
    for ti in range(NT):
        b = ti % 2
        for c in range(DC):
            sl = slots[c % 4]
            w2 = list(w0) if ti == 0 and c == 0 else []
            if slot_evac[c % 4] is not None:
                w2.append(slot_evac[c % 4])
            if c == 0 and ti >= 2:
                w2.append(store_done[b])
            em.op("tensor", lambda t, c=c, ti=ti, sl=sl: t.transpose(
                sl, P.XT[c][:, ti * 128:(ti + 1) * 128], idn[:]),
                w=w2, s=P.st)
            tw = P.st.at()
            em.op("vector", lambda v, c=c, b=b, sl=sl: v.tensor_copy(
                ring[b][:, c * 128:(c + 1) * 128], sl),
                w=[tw], s=P.sv)
            slot_evac[c % 4] = P.sv.at()
        em.op("sync", lambda y, ti=ti, b=b: y.dma_start(
            out=x_out[ti * 128:(ti + 1) * 128, :], in_=ring[b][:]),
            w=[P.sv.at()], s=sst[b], k=16)
        store_done[b] = sst[b].at()
    em.barrier_wait("sync", [sst[0].at(), sst[1].at()])


def emit_ln(em, P, C, sb):
    ones_sum, ones_bc = C["ones_sum"], C["ones_bc"]
    xc, xq = sb["xc"], sb["xq"]
    rows = sb["rows"]
    rbf = sb["rbf"]
    AB, BB, t1 = sb["AB"], sb["BB"], sb["t1"]
    r0, r1, r2, r3 = rows[0:1, 0:512], rows[0:1, 512:1024], rows[0:1, 1024:1536], rows[0:1, 1536:2048]
    abf, bbf = rbf[0:1, 0:512], rbf[0:1, 512:1024]

    xtw = getattr(P, "xt_ready", None)
    em.op("vector", lambda v: v.memset(rbf[0:4, :], 0.0))
    mm_hist = []
    row_done = None
    bc_done = None
    for tb in range(NTB):
        for c in range(DC):
            h = c % 2
            gi = tb * DC + c
            w_cast = [mm_hist[gi - 2]] if gi >= 2 else []
            if xtw is not None and tb == 0 and c == 0:
                w_cast = [xtw] + w_cast
            if row_done is not None and c == 0:
                w_cast.append(row_done)
            em.op("scalar", lambda s, c=c, tb=tb, h=h: s.copy(
                xc[:, h * 512:(h + 1) * 512], P.XT[c][:, tb * TB:(tb + 1) * TB]), w=w_cast, s=P.ss)
            em.op("scalar", lambda s, c=c, tb=tb, h=h: s.activation(
                xq[:, h * 512:(h + 1) * 512], P.XT[c][:, tb * TB:(tb + 1) * TB], AF.Square), s=P.ss)
            cw = P.ss.at()
            wmm = [cw]
            if c == 0 and tb > 0:
                wmm.append(ev_done)
            em.op("tensor", lambda t, c=c, h=h: t.matmul(
                P.PA[0:1, 0:512], ones_sum[:, 0:1], xc[:, h * 512:(h + 1) * 512],
                start=(c == 0), stop=(c == DC - 1)), w=wmm, s=None)
            em.op("tensor", lambda t, c=c, h=h: t.matmul(
                P.PA[0:1, 512:1024], ones_sum[:, 0:1], xq[:, h * 512:(h + 1) * 512],
                start=(c == 0), stop=(c == DC - 1)), s=P.st)
            mm_hist.append(P.st.at())
        mmw = P.st.at()
        em.op("scalar", lambda s: s.copy(r0, P.PA[0:1, 0:512]), w=[mmw], s=P.ss)
        em.op("scalar", lambda s: s.copy(r1, P.PA[0:1, 512:1024]), s=P.ss)
        ev_done = P.ss.at()
        em.op("vector", lambda v: v.tensor_scalar(r0, r0, 1.0 / D, None, ALU.mult), w=[ev_done])
        em.op("vector", lambda v: v.tensor_scalar(r1, r1, 1.0 / D, None, ALU.mult))
        em.op("vector", lambda v: v.scalar_tensor_tensor(r2, r0, 1.0, r0, ALU.mult, ALU.mult))
        vfence(em, P, r2)
        em.op("vector", lambda v: v.tensor_tensor(r1, r1, r2, ALU.subtract), s=P.sv)
        em.op("scalar", lambda s: s.activation(r1, r1, AF.Sqrt, bias=C["eps"][0:1, 0:1], scale=1.0),
              w=[P.sv.at()], s=P.ss)
        em.op("vector", lambda v: v.reciprocal(r3, r1), w=[P.ss.at()])
        vfence(em, P, r3)
        em.op("vector", lambda v: v.scalar_tensor_tensor(r2, r0, -1.0, r3, ALU.mult, ALU.mult))
        em.op("vector", lambda v: v.tensor_copy(abf, r3))
        em.op("vector", lambda v: v.tensor_copy(bbf, r2), s=P.sv)
        row_done = P.sv.at()
        wb0 = [row_done]
        if bc_done is not None:
            wb0.append(bc_done)
        bc4 = C["bc4"]
        em.op("tensor", lambda t, bc4=bc4: t.matmul(
            P.PB[:, 0:512], bc4[0:4, :], rbf[0:4, 0:512], start=True, stop=True), w=wb0, s=P.st)
        em.op("tensor", lambda t, bc4=bc4: t.matmul(
            P.PB[:, 512:1024], bc4[0:4, :], rbf[0:4, 512:1024], start=True, stop=True), s=P.st)
        tw = P.st.at()
        em.op("scalar", lambda s, tb=tb: s.copy(AB[:, tb * TB:(tb + 1) * TB], P.PB[:, 0:512]),
              w=[tw], s=P.ss)
        em.op("scalar", lambda s, tb=tb: s.copy(BB[:, tb * TB:(tb + 1) * TB], P.PB[:, 512:1024]),
              s=P.ss)
        bc_done = P.ss.at()
    pairs = [(c, tb) for c in range(DC) for tb in range(NTB)]
    for i in range(0, len(pairs), 2):
        for j, (c, tb) in enumerate(pairs[i:i + 2]):
            w = [bc_done] if (c == 0 and tb == 0) else ()
            em.op("vector", lambda v, c=c, tb=tb, j=j: v.tensor_tensor(
                t1[:, j * TB:(j + 1) * TB], P.XT[c][:, tb * TB:(tb + 1) * TB],
                AB[:, tb * TB:(tb + 1) * TB], ALU.mult), w=w)
        for j, (c, tb) in enumerate(pairs[i:i + 2]):
            em.op("vector", lambda v, c=c, tb=tb, j=j: v.tensor_tensor(
                P.XH[c][:, tb * TB:(tb + 1) * TB], t1[:, j * TB:(j + 1) * TB],
                BB[:, tb * TB:(tb + 1) * TB], ALU.add), s=P.sv)
    vfence(em, P, P.XHBIG[:, DC * T - 256:DC * T], s_=P.sv)
    P.xh_ready = P.sv.at()


def make_ln_scratch(em):
    return {
        "xc": sbuf(em, "ln_xc", [128, 1024], BF16),
        "xq": sbuf(em, "ln_xq", [128, 1024], BF16),
        "rows": sbuf(em, "ln_rows", [1, 2048], F32),
        "rbf": sbuf(em, "ln_rbf", [4, 1024], BF16),
        "AB": sbuf(em, "ln_AB", [128, T], BF16),
        "BB": sbuf(em, "ln_BB", [128, T], BF16),
        "t1": sbuf(em, "ln_t1", [128, 1024], F32),
    }


def emit_ffn(em, P, C, W, layer):
    with contextlib.ExitStack() as fscope:
        outer, em.stack = em.stack, fscope
        try:
            emit_ffn_body(em, P, C, W, layer)
        finally:
            em.stack = outer


def emit_ffn_body(em, P, C, W, layer):
    STAGE = os.environ.get("KSTAGE", "all")
    if layer == 0:
        E, NCHUNK, HCS, SLABS, HTOT = E0, 22, HC0, SLAB0, H0
        act_fn, gated = AF.Silu, True
    else:
        E, NCHUNK, HCS, SLABS, HTOT = E1, 16, [128] * 16, SLAB1, F1
        act_fn, gated = AF.Gelu, False
    MAXCH = max(j1 - j0 for j0, j1 in SLABS)
    hlen_max = MAXCH * 128
    idn = C["idn"]
    esel_d = C["esel0_d"] if layer == 0 else C["esel1_d"]
    esel = sbuf(em, "esel", [E, E * 128], BF16)
    sesel = em.sem("sesel")
    em.op("gpsimd", lambda g, esel_d=esel_d: g.dma_start(out=esel[:], in_=esel_d),
          s=sesel, k=16)
    esel_w = sesel.at()
    wg, wu, wd = (W["gate_w"], W["up_w"], W["down_w"]) if gated else (W["w1"], None, W["w2"])
    wgb, wub, wdb = (W["gate_b"], W["up_b"], W["down_b"]) if gated else (W["b1"], None, W["b2"])
    wrouter = W["router_w"]

    gbT = sbuf(em, "gbT", [128, NCHUNK * E], F32)
    ubT = sbuf(em, "ubT", [128, NCHUNK * E], F32) if gated else None
    db_sb = sbuf(em, "db_sb", [E, D], BF16)
    rw_sb = sbuf(em, "rw_sb", [128, DC * E], BF16)
    mtT = sbuf(em, "mtT", [E, T], BF16)
    wbc = sbuf(em, "wbc", [128, T], BF16)
    plf = sbuf(em, "plf", [128, NT * E], F32)
    r_m1 = sbuf(em, "r_m1", [128, NT], F32)
    r_m2 = sbuf(em, "r_m2", [128, NT], F32)
    r_sc1 = sbuf(em, "r_sc1", [128, NT * E], F32)
    r_sc2 = sbuf(em, "r_sc2", [128, NT * E], F32)
    r_mt = sbuf(em, "r_mt", [128, NT * E], F32)
    r_den = sbuf(em, "r_den", [128, NT], F32)

    stab = em.sem("stab")
    em.op("gpsimd", lambda g: g.dma_start(out=db_sb[:], in_=wdb), s=stab, k=16)
    em.op("gpsimd", lambda g: g.dma_start(
        out=rw_sb[:].rearrange("k (c e) -> k c e", c=DC),
        in_=wrouter.rearrange("(c k) e -> k c e", c=DC)), s=stab, k=16)
    small_w = stab.at()
    if STAGE == "tables0":
        return

    with contextlib.ExitStack() as scope:
        em2_stack, em.stack = em.stack, scope
        gb_raw = sbuf(em, "gb_raw", [E, NCHUNK * 128], F32)
        ub_raw = sbuf(em, "ub_raw", [E, NCHUNK * 128], F32) if gated else None
        sbr = em.sem("sbraw")
        if HTOT < NCHUNK * 128:
            em.op("vector", lambda v: v.memset(gb_raw[:], 0.0), s=P.sv)
            if gated:
                em.op("vector", lambda v: v.memset(ub_raw[:], 0.0), s=P.sv)
        mz = P.sv.at()
        em.op("sync", lambda y: y.dma_start(out=gb_raw[0:E, 0:HTOT], in_=wgb), w=[mz], s=sbr, k=16)
        if gated:
            em.op("sync", lambda y: y.dma_start(out=ub_raw[0:E, 0:HTOT], in_=wub), s=sbr, k=16)
        braww = sbr.at()
        for bi in range(2 if gated else 1):
            src = gb_raw if bi == 0 else ub_raw
            dstT = gbT if bi == 0 else ubT
            for j in range(NCHUNK):
                em.op("tensor", lambda t, j=j, src=src: t.transpose(
                    P.PM[0:128, j * E:(j + 1) * E], src[0:E, j * 128:(j + 1) * 128], idn[0:E, 0:E]),
                    w=([braww, (P.ss, P.ss.n)] if j == 0 else []),
                    s=(P.st if j == NCHUNK - 1 else None))
            em.op("scalar", lambda s, dstT=dstT: s.copy(dstT[:], P.PM[:, 0:NCHUNK * E]),
                  w=[P.st.at()], s=P.ss)
        biasT_pe = P.st.at()
        biasT_done = P.ss.at()
        if STAGE == "tables":
            em.stack = em2_stack
            return
        ln_sb = make_ln_scratch(em)
        C["dbg_ln_sb"] = ln_sb
        emit_ln(em, P, C, ln_sb)
        em.stack = em2_stack
    xhw = P.xh_ready
    if STAGE == "ln":
        for c in range(DC):
            em.op("vector", lambda v, c=c: v.tensor_copy(P.XT[c], P.XH[c]),
                  w=[xhw] if c == 0 else (), s=P.sv)
        return
    alias_gate = [xhw, biasT_pe, biasT_done]

    for ti in range(NT):
        for c in range(DC):
            w = [xhw, small_w, biasT_done] if (ti == 0 and c == 0) else []
            em.op("tensor", lambda t, ti=ti, c=c: t.matmul(
                P.PM[0:128, ti * E:(ti + 1) * E],
                P.XH[c][:, ti * 128:(ti + 1) * 128],
                rw_sb[:, c * E:(c + 1) * E],
                start=(c == 0), stop=(c == DC - 1)),
                w=w, s=None)
    em.op("tensor", lambda t: t.matmul(
        P.PM[0:128, 300:300 + E], P.XH[0][:, 0:128], rw_sb[:, 0:E], start=True, stop=True), s=P.st)
    em.op("scalar", lambda s: s.copy(plf[:], P.PM[:, 0:NT * E]), w=[P.st.at()], s=None)
    sfence(em, P, plf[:], s_=P.ss)
    plw = P.ss.at()

    pl3 = plf[:].rearrange("p (t e) -> p t e", e=E)
    m1b = r_m1[:].rearrange("p (t o) -> p t o", o=1).broadcast_to([128, NT, E])
    m2b = r_m2[:].rearrange("p (t o) -> p t o", o=1).broadcast_to([128, NT, E])
    em.op("vector", lambda v: v.reduce_max(r_m1[:], pl3, axis=AX.X), w=[plw])
    vfence(em, P, r_m1[:])
    if layer == 0:
        em.op("vector", lambda v: v.tensor_tensor(
            r_sc1[:].rearrange("p (t e) -> p t e", e=E), pl3, m1b, ALU.is_ge))
        vfence(em, P, r_sc1[:])
        em.op("vector", lambda v: v.scalar_tensor_tensor(r_sc2[:], r_sc1[:], NEG, plf[:], ALU.mult, ALU.add))
        vfence(em, P, r_sc2[:])
        em.op("vector", lambda v: v.reduce_max(r_m2[:], r_sc2[:].rearrange("p (t e) -> p t e", e=E), axis=AX.X))
        vfence(em, P, r_m2[:])
        em.op("vector", lambda v: v.tensor_tensor(
            r_sc2[:].rearrange("p (t e) -> p t e", e=E), pl3, m2b, ALU.is_ge))
        vfence(em, P, r_sc2[:])
        em.op("vector", lambda v: v.tensor_tensor(
            r_sc1[:].rearrange("p (t e) -> p t e", e=E), pl3, m1b, ALU.subtract))
        vfence(em, P, r_sc1[:], s_=P.sv)
        em.op("scalar", lambda s: s.activation(r_sc1[:], r_sc1[:], AF.Exp), w=[P.sv.at()], s=None)
        sfence(em, P, r_sc1[:], s_=P.ss)
        em.op("vector", lambda v: v.tensor_tensor(r_sc1[:], r_sc1[:], r_sc2[:], ALU.mult),
              w=[P.ss.at()])
        vfence(em, P, r_sc1[:])
        em.op("vector", lambda v: v.reduce_sum(r_den[:], r_sc1[:].rearrange("p (t e) -> p t e", e=E), axis=AX.X))
        vfence(em, P, r_den[:])
        em.op("vector", lambda v: v.reciprocal(r_m2[:], r_den[:]))
        vfence(em, P, r_m2[:])
        em.op("vector", lambda v: v.tensor_scalar(r_m2[:], r_m2[:], 0.5, None, ALU.mult))
        vfence(em, P, r_m2[:])
        em.op("vector", lambda v: v.tensor_tensor(
            r_mt[:].rearrange("p (t e) -> p t e", e=E), r_sc1[:].rearrange("p (t e) -> p t e", e=E),
            m2b, ALU.mult))
        vfence(em, P, r_mt[:], s_=P.sv)
    else:
        em.op("vector", lambda v: v.tensor_tensor(
            r_sc1[:].rearrange("p (t e) -> p t e", e=E), pl3, m1b, ALU.subtract))
        vfence(em, P, r_sc1[:], s_=P.sv)
        em.op("scalar", lambda s: s.activation(r_sc1[:], r_sc1[:], AF.Exp), w=[P.sv.at()], s=None)
        sfence(em, P, r_sc1[:], s_=P.ss)
        em.op("vector", lambda v: v.reduce_sum(r_den[:], r_sc1[:].rearrange("p (t e) -> p t e", e=E), axis=AX.X),
              w=[P.ss.at()])
        vfence(em, P, r_den[:])
        em.op("vector", lambda v: v.reciprocal(r_m2[:], r_den[:]))
        vfence(em, P, r_m2[:])
        em.op("vector", lambda v: v.tensor_scalar(r_m2[:], r_m2[:], 0.5, None, ALU.mult))
        vfence(em, P, r_m2[:])
        em.op("vector", lambda v: v.tensor_tensor(
            r_sc2[:].rearrange("p (t e) -> p t e", e=E), pl3, m1b, ALU.is_ge))
        vfence(em, P, r_sc2[:])
        em.op("vector", lambda v: v.tensor_tensor(
            r_mt[:].rearrange("p (t e) -> p t e", e=E), r_sc2[:].rearrange("p (t e) -> p t e", e=E),
            m2b, ALU.mult))
        vfence(em, P, r_mt[:], s_=P.sv)
    mt_done = P.sv.at()
    if STAGE == "dbg_direct":
        dst = P.dbg_out
        em.op("sync", lambda y: y.dma_start(out=dst[0:128, 0:NT * E], in_=plf[:]), w=[plw], s=P.syd, k=16)
        em.op("gpsimd", lambda y: y.dma_start(out=dst[128:256, 0:T], in_=P.XH[0]), w=[xhw], s=P.sgd, k=16)
        em.op("sync", lambda y: y.dma_start(out=dst[256:384, 0:NT * E], in_=r_mt[:]), w=[mt_done], s=P.syd, k=16)
        em.op("gpsimd", lambda y: y.dma_start(out=dst[384:512, 0:DC * E], in_=rw_sb[:]), w=[small_w], s=P.sgd, k=16)
        em.op("sync", lambda y: y.dma_start(out=dst[512:640, 0:T], in_=P.XT[0][:]), w=[xhw], s=P.syd, k=16)
        ln_sb2 = C["dbg_ln_sb"]
        em.op("gpsimd", lambda y: y.dma_start(out=dst[640:768, 0:T], in_=ln_sb2["AB"][:]), w=[xhw], s=P.sgd, k=16)
        em.op("gpsimd", lambda y: y.dma_start(out=dst[768:896, 0:T], in_=ln_sb2["BB"][:]), w=[xhw], s=P.sgd, k=16)
        em.barrier_wait("sync", [P.syd.at()])
        em.barrier_wait("gpsimd", [P.sgd.at()])
        P.dbg_done = True
        return
    if STAGE == "routing":
        return

    per_round = 512 // 128
    prev_evac = None
    for r0 in range(0, NT, per_round):
        for k in range(per_round):
            ti = r0 + k
            w = [mt_done] if (r0 == 0 and k == 0) else ([prev_evac] if (k == 0 and prev_evac) else [])
            em.op("tensor", lambda t, ti=ti, k=k: t.transpose(
                P.PM[0:E, k * 128:(k + 1) * 128], r_mt[:, ti * E:(ti + 1) * E], idn[:]),
                w=w, s=(P.st if k == per_round - 1 else None))
        em.op("scalar", lambda s, r0=r0: s.copy(
            mtT[0:E, r0 * 128:(r0 + per_round) * 128], P.PM[0:E, 0:per_round * 128]),
            w=[P.st.at()], s=P.ss)
        prev_evac = P.ss.at()
    mtT_done = P.ss.at()
    if STAGE == "dbg_mtt":
        em.op("vector", lambda v: v.tensor_copy(P.XTBIG[0:E, 0:T], mtT[0:E, :]), w=[mtT_done], s=P.sv)
        em.op("vector", lambda v: v.tensor_copy(P.XTBIG[:, T:T + NT * E], r_mt[:]), s=P.sv)
        em.op("vector", lambda v: v.tensor_copy(P.XTBIG[:, 2 * T:2 * T + NT * E], plf[:]), s=P.sv)
        em.op("vector", lambda v: v.tensor_copy(P.XTBIG[:, 3 * T:3 * T + NT], r_m2[:]), s=P.sv)
        em.op("vector", lambda v: v.tensor_copy(P.XTBIG[:, 6 * T:6 * T + DC * E], rw_sb[:]), s=P.sv)
        em.op("vector", lambda v: v.tensor_copy(P.XTBIG[:, 7 * T:8 * T], P.XH[0][:]), s=P.sv)
        em.op("vector", lambda v: v.tensor_copy(P.XTBIG[:, 4 * T:4 * T + NT * E], r_sc1[:]), s=P.sv)
        em.op("vector", lambda v: v.tensor_copy(P.XTBIG[:, 5 * T:5 * T + NT * E], r_sc2[:]), s=P.sv)
        return
    if STAGE == "mtt":
        return

    gw_buf = [sbuf(em, f"gw{i}", [128, DC * hlen_max], BF16) for i in range(2)]
    uw_buf = [sbuf(em, f"uw{i}", [128, DC * hlen_max], BF16) for i in range(2)] if gated else [None, None]
    dw_buf = [sbuf(em, f"dw{i}", [128, MAXCH * D], BF16) for i in range(2)]
    gu_g = sbuf(em, "gu_g", [128, 1024], BF16)
    gu_u = sbuf(em, "gu_u", [128, 1024], BF16) if gated else None
    hs_buf = sbuf(em, "hs", [128, MAXCH * TB], BF16)

    slabs = [(e, j0, j1) for e in range(E) for (j0, j1) in SLABS]
    if STAGE.startswith("slab"):
        slabs = slabs[: int(STAGE[4:])]
    sgw = [em.sem("sgw0"), em.sem("sgw1")]
    buf_free_t = [None, None]
    wready = [None, None]
    wbc_used = None
    hs_free = None
    py_evac = [(P.sv, 0), (P.sv, 0)]
    gu_evac = [(P.ss, 0), (P.ss, 0)]
    gu_used = [(P.sv, 0), (P.sv, 0)]
    last_e = None
    wbc_ready = None

    for si, (e, j0, j1) in enumerate(slabs):
        buf = si % 2
        nch = j1 - j0
        hlo, hhi = j0 * 128, min(j1 * 128, HTOT)
        hlen = hhi - hlo
        wfree = list(alias_gate) if si < 2 else [buf_free_t[buf]]
        em.op("gpsimd", lambda g, e=e, hlo=hlo, hhi=hhi, hlen=hlen, buf=buf: g.dma_start(
            out=gw_buf[buf][:, 0:DC * hlen].rearrange("k (c h) -> k c h", c=DC),
            in_=wg[e, :, hlo:hhi].rearrange("(c k) h -> k c h", c=DC)),
            w=wfree, s=sgw[buf], k=16)
        if gated:
            em.op("gpsimd", lambda g, e=e, hlo=hlo, hhi=hhi, hlen=hlen, buf=buf: g.dma_start(
                out=uw_buf[buf][:, 0:DC * hlen].rearrange("k (c h) -> k c h", c=DC),
                in_=wu[e, :, hlo:hhi].rearrange("(c k) h -> k c h", c=DC)),
                s=sgw[buf], k=16)
        for jj in range(nch):
            j = j0 + jj
            nr = HCS[j]
            em.op("gpsimd", lambda g, e=e, j=j, jj=jj, nr=nr, buf=buf: g.dma_start(
                out=dw_buf[buf][0:nr, jj * D:(jj + 1) * D],
                in_=wd[e, j * 128:j * 128 + nr, :]), s=sgw[buf], k=16)
        wready[buf] = sgw[buf].at()

        if e != last_e:
            for tb in range(NTB):
                w = [mtT_done, esel_w] if tb == 0 else [(P.ss, P.ss.n)]
                if tb == 0 and wbc_used is not None:
                    w.append(wbc_used)
                em.op("tensor", lambda t, e=e, tb=tb: t.matmul(
                    P.PM[:, 0:512], esel[0:E, e * 128:(e + 1) * 128],
                    mtT[0:E, tb * TB:(tb + 1) * TB], start=True, stop=True),
                    w=w, s=P.st)
                em.op("scalar", lambda s, tb=tb: s.copy(
                    wbc[:, tb * TB:(tb + 1) * TB], P.PM[:, 0:512]), w=[P.st.at()], s=P.ss)
            wbc_ready = P.ss.at()
            last_e = e

        for tb in range(NTB):
            for jj in range(nch):
                j = j0 + jj
                nr = HCS[j]
                half = jj % 2
                pg = P.PA[:, half * 512:(half + 1) * 512]
                pu = P.PB[:, half * 512:(half + 1) * 512] if gated else None
                w = [gu_evac[half]]
                if tb == 0 and jj == 0:
                    w += [wready[buf], xhw]
                for c in range(DC):
                    em.op("tensor", lambda t, c=c, jj=jj, nr=nr, tb=tb, buf=buf, pg=pg, hlen=hlen: t.matmul(
                        pg[0:nr, :], gw_buf[buf][:, c * hlen + jj * 128:c * hlen + jj * 128 + nr],
                        P.XH[c][:, tb * TB:(tb + 1) * TB],
                        start=(c == 0), stop=(c == DC - 1)),
                        w=w if c == 0 else (), s=(P.st if (c == DC - 1 and not gated) else None))
                if gated:
                    for c in range(DC):
                        em.op("tensor", lambda t, c=c, jj=jj, nr=nr, tb=tb, buf=buf, pu=pu, hlen=hlen: t.matmul(
                            pu[0:nr, :], uw_buf[buf][:, c * hlen + jj * 128:c * hlen + jj * 128 + nr],
                            P.XH[c][:, tb * TB:(tb + 1) * TB],
                            start=(c == 0), stop=(c == DC - 1)),
                            s=(P.st if c == DC - 1 else None))
                mmw = P.st.at()
                em.op("scalar", lambda s, nr=nr, half=half, j=j, e=e, pg=pg: s.activation(
                    gu_g[0:nr, half * 512:(half + 1) * 512], pg[0:nr, :], act_fn,
                    bias=gbT[0:nr, j * E + e:j * E + e + 1], scale=1.0),
                    w=[mmw, gu_used[half]], s=(None if gated else P.ss))
                if gated:
                    em.op("scalar", lambda s, nr=nr, half=half, j=j, e=e, pu=pu: s.activation(
                        gu_u[0:nr, half * 512:(half + 1) * 512], pu[0:nr, :], AF.Identity,
                        bias=ubT[0:nr, j * E + e:j * E + e + 1], scale=1.0), s=P.ss)
                gu_evac[half] = P.ss.at()
                wv = [gu_evac[half], wbc_ready]
                if jj == 0 and hs_free is not None:
                    wv.append(hs_free)
                if gated:
                    em.op("vector", lambda v, nr=nr, half=half: v.tensor_tensor(
                        gu_g[0:nr, half * 512:(half + 1) * 512],
                        gu_g[0:nr, half * 512:(half + 1) * 512],
                        gu_u[0:nr, half * 512:(half + 1) * 512], ALU.mult), w=wv)
                    em.op("vector", lambda v, nr=nr, half=half, jj=jj, tb=tb: v.tensor_tensor(
                        hs_buf[0:nr, jj * TB:(jj + 1) * TB],
                        gu_g[0:nr, half * 512:(half + 1) * 512],
                        wbc[0:nr, tb * TB:(tb + 1) * TB], ALU.mult), s=P.sv)
                else:
                    em.op("vector", lambda v, nr=nr, half=half, jj=jj, tb=tb: v.tensor_tensor(
                        hs_buf[0:nr, jj * TB:(jj + 1) * TB],
                        gu_g[0:nr, half * 512:(half + 1) * 512],
                        wbc[0:nr, tb * TB:(tb + 1) * TB], ALU.mult), w=wv, s=P.sv)
                gu_used[half] = P.sv.at()
            hs_done = P.sv.at()
            wbc_used = hs_done
            for cc in range(DC):
                half = cc % 2
                py = P.PY[:, half * 512:(half + 1) * 512]
                first = True
                w = [hs_done, py_evac[half]]
                if si == 0:
                    em.op("tensor", lambda t, cc=cc, tb=tb, py=py: t.matmul(
                        py, db_sb[0:E, cc * 128:(cc + 1) * 128], mtT[0:E, tb * TB:(tb + 1) * TB],
                        start=True, stop=False), w=w + [small_w, mtT_done], s=None)
                    first = False
                    w = []
                for jj in range(nch):
                    j = j0 + jj
                    nr = HCS[j]
                    em.op("tensor", lambda t, jj=jj, nr=nr, cc=cc, buf=buf, py=py, st=(first and jj == 0), sp=(jj == nch - 1): t.matmul(
                        py, dw_buf[buf][0:nr, jj * D + cc * 128:jj * D + cc * 128 + 128],
                        hs_buf[0:nr, jj * TB:(jj + 1) * TB],
                        start=st, stop=sp),
                        w=w if jj == 0 else (), s=(P.st if jj == nch - 1 else None))
                    w = []
                    first = False
                mmw = P.st.at()
                em.op("vector", lambda v, cc=cc, tb=tb, py=py: v.tensor_tensor(
                    P.XT[cc][:, tb * TB:(tb + 1) * TB], P.XT[cc][:, tb * TB:(tb + 1) * TB],
                    py, ALU.add), w=[mmw], s=P.sv)
                py_evac[half] = P.sv.at()
            hs_free = P.st.at()
        buf_free_t[buf] = P.st.at()
    P.ffn_done = P.sv.at()


def emit_attn(em, P, C, W, layer, xsrc):
    nc = em.nc
    idn_bf = C["idn_bf"]
    bc4 = C["bc4"]
    QB = 4
    scale = 1.0 / 8.0

    KN = D * T
    VN = T * NH * 65
    if not hasattr(P, "kv_in"):
        P.kv_in = nc.dram_tensor("kv_in", [KN + VN], BF16).ap()
        P.kv_out = nc.dram_tensor("kv_out", [2 * (KN + VN)], BF16).ap()
    kv_in, kv_out = P.kv_in, P.kv_out

    def kin_rows(r0, nr):
        return kv_in[r0 * T:(r0 + nr) * T].rearrange("(p t) -> p t", t=T)

    def vin_rows(r0, nr):
        return kv_in[KN + r0 * NH * 65: KN + (r0 + nr) * NH * 65].rearrange(
            "(p x) -> p x", x=NH * 65)

    def kout_rows(rank, r0, nr):
        b = rank * (KN + VN) + r0 * T
        return kv_out[b:b + nr * T].rearrange("(p t) -> p t", t=T)

    def vout_rows(rank, r0, nr):
        b = rank * (KN + VN) + KN + r0 * NH * 65
        return kv_out[b:b + nr * NH * 65].rearrange("(p x) -> p x", x=NH * 65)

    with contextlib.ExitStack() as scope:
        outer, em.stack = em.stack, scope
        raw = [sbuf(em, f"araw{i}", [128, 3072 if layer == 0 else 1024], BF16) for i in range(2)]
        rotQ = sbuf(em, "rotQ", [128, DC * T], BF16)
        with contextlib.ExitStack() as scope_ln:
            em.stack = scope_ln
            ln_sb = make_ln_scratch(em)
            emit_ln(em, P, C, ln_sb)
            em.stack = scope
        xhw = P.xh_ready

        scope_w = contextlib.ExitStack()
        em.stack = scope_w
        swq = em.sem("swq")
        if layer == 0:
            wq_sb = sbuf(em, "wq_sb", [128, DC * 1024], BF16)
            wk_sb = sbuf(em, "wk_sb", [128, DC * 1024], BF16)
            wv_sb = sbuf(em, "wv_sb", [128, DC * 1024], BF16)
            wqkv = W["W_qkv"]
            raw_used = [None, None]
            for c in range(DC):
                b = c % 2
                wld = [raw_used[b]] if raw_used[b] is not None else [xhw]
                em.op("gpsimd", lambda g, c=c, b=b: g.dma_start(
                    out=raw[b][:], in_=wqkv[c * 128:(c + 1) * 128, :]), w=wld, s=swq, k=16)
                dw = swq.at()
                for wi, dst in enumerate((wq_sb, wk_sb, wv_sb)):
                    em.op("vector", lambda v, c=c, b=b, wi=wi, dst=dst: v.tensor_copy(
                        dst[:, c * 1024:(c + 1) * 1024],
                        raw[b][:].rearrange("p (n s) -> p n s", s=3)[:, :, wi]),
                        w=[dw] if wi == 0 else (), s=(P.sv if wi == 2 else None))
                raw_used[b] = P.sv.at()
            qkvw_done = P.sv.at()
            wsel = {"q": wq_sb, "k": wk_sb, "v": wv_sb}
        else:
            wT = sbuf(em, "wT", [128, 64 * 128], BF16)
            raw_used = [None, None]
            tdone = [None, None]

            def prep_matrix(wnm, rope_perm, gate):
                wm = W[wnm]
                for fc in range(DC):
                    b = fc % 2
                    wld = list(gate) if fc == 0 else []
                    if raw_used[b] is not None:
                        wld.append(raw_used[b])
                    if rope_perm:
                        wv4 = wm.rearrange("(h dk2 two) d -> h two dk2 d", dk2=32, two=2)
                        for pi, (hl, sgn) in enumerate(
                                ((0, 0), (0, 1), (1, 0), (1, 1))):
                            src = wv4[2 * fc + hl, sgn]
                            dst = raw[b][hl * 64 + sgn * 32: hl * 64 + sgn * 32 + 32, :]
                            em.op("gpsimd", lambda g, src=src, dst=dst: g.dma_start(
                                out=dst, in_=src), w=wld if pi == 0 else [],
                                s=swq, k=16)
                    else:
                        src = wm[fc * 128:(fc + 1) * 128, :]
                        em.op("gpsimd", lambda g, src=src, b=b: g.dma_start(
                            out=raw[b][:], in_=src), w=wld, s=swq, k=16)
                    dw = swq.at()
                    for dc in range(DC):
                        pm = P.PM[:, 0:128] if dc % 2 == 0 else P.P8[:, 0:128]
                        wv_ = [dw] if dc == 0 else []
                        if tdone[dc % 2] is not None:
                            wv_.append(tdone[dc % 2])
                        em.op("tensor", lambda t, b=b, dc=dc, pm=pm: t.matmul(
                            pm, raw[b][:, dc * 128:(dc + 1) * 128], idn_bf[:],
                            start=True, stop=True), w=wv_, s=P.st)
                        tw = P.st.at()
                        em.op("scalar", lambda sc, dc=dc, fc=fc, pm=pm: sc.copy(
                            wT[:, (dc * DC + fc) * 128:(dc * DC + fc) * 128 + 128], pm),
                            w=[tw], s=P.ss)
                        tdone[dc % 2] = P.ss.at()
                    raw_used[b] = P.ss.at()
                return P.ss.at()
            qkvw_done = None
            wsel = {"q": wT, "k": wT, "v": wT}
            cosr = sbuf(em, "cosr", [32, T], F32)
            sinr = sbuf(em, "sinr", [32, T], F32)
            srt = em.sem("srt")
            cd, sd_ = C["cos_d"], C["sin_d"]
            em.op("sync", lambda y, cd=cd: y.dma_start(out=cosr[:], in_=cd),
                  w=[xhw], s=srt, k=16)
            em.op("sync", lambda y, sd_=sd_: y.dma_start(out=sinr[:], in_=sd_), s=srt, k=16)
            rope_w = srt.at()
            rp1 = sbuf(em, "rp1", [32, 512], F32)
            rp2 = sbuf(em, "rp2", [32, 512], F32)
            rp3 = sbuf(em, "rp3", [32, 512], F32)
            rp4 = sbuf(em, "rp4", [32, 512], F32)

        def wslice(which, dc, fc):
            if layer == 0:
                return wsel[which][:, dc * 1024 + fc * 128: dc * 1024 + (fc + 1) * 128]
            return wsel[which][:, (dc * DC + fc) * 128:(dc * DC + fc) * 128 + 128]

        def vslice(dc, fb):
            if layer == 0:
                return wv_sb[:, dc * 1024 + fb * 512: dc * 1024 + (fb + 1) * 512]
            return wT[:, (dc * DC + 4 * fb) * 128:(dc * DC + 4 * fb + 4) * 128]

        krow = [sbuf(em, f"krow{i}", [128, T], BF16) for i in range(2)]
        vrow = [sbuf(em, f"vrow{i}", [128, NH * 65], BF16) for i in range(2)]
        skst = [em.sem("skst0"), em.sem("skst1")]
        svst = [em.sem("svst0"), em.sem("svst1")]
        em.op("gpsimd", lambda g: g.memset(vrow[0][:], 1.0), s=P.sg)
        em.op("gpsimd", lambda g: g.memset(vrow[1][:], 1.0), s=P.sg)
        vrow_ones = P.sg.at()

        def emit_rope(dst_ap, psrc, tb, w_first, s_last):
            cs = cosr[0:32, tb * TB:(tb + 1) * TB]
            sn = sinr[0:32, tb * TB:(tb + 1) * TB]
            qe = [psrc[0:32, :], psrc[64:96, :]]
            qo = [psrc[32:64, :], psrc[96:128, :]]
            rp = [rp1, rp2, rp3, rp4]
            for pass_ in range(2):
                f1 = cs if pass_ == 0 else sn
                f2 = sn if pass_ == 0 else cs
                alu = ALU.subtract if pass_ == 0 else ALU.add
                for hh in range(2):
                    em.op("vector", lambda v, hh=hh, f1=f1: v.tensor_tensor(
                        rp[2 * hh][:], qe[hh], f1, ALU.mult),
                        w=w_first if (pass_ == 0 and hh == 0) else ())
                    em.op("vector", lambda v, hh=hh, f2=f2: v.tensor_tensor(
                        rp[2 * hh + 1][:], qo[hh], f2, ALU.mult))
                for hh in range(2):
                    b0 = hh * 64 + (0 if pass_ == 0 else 32)
                    em.op("vector", lambda v, dst_ap=dst_ap, b0=b0, hh=hh, alu=alu: v.tensor_tensor(
                        dst_ap[b0:b0 + 32, :], rp[2 * hh][:], rp[2 * hh + 1][:], alu),
                        s=(s_last if pass_ == 1 and hh == 1 else None))

        qbank = [P.PA[:, 0:512], P.PA[:, 512:1024], P.PB[:, 0:512], P.PB[:, 512:1024]]
        pa_ev4q = [(P.ss, 0)] * 4
        kst_hist = [None, None]
        proj_done = {}
        for which in ("q", "k"):
            if layer == 1:
                gate = [xhw] if which == "q" else [proj_done["q"]]
                wT_w = prep_matrix("wq" if which == "q" else "wk", True, gate)
            for fc in range(DC):
                b = fc % 2
                for tb in range(NTB):
                    pq = qbank[tb]
                    wv_ = [pa_ev4q[tb]]
                    if fc == 0 and tb == 0:
                        if layer == 1:
                            wv_.append(wT_w)
                        if which == "q":
                            wv_ += [xhw]
                            if layer == 0:
                                wv_.append(qkvw_done)
                            if layer == 1:
                                wv_.append(rope_w)
                    for dc in range(DC):
                        em.op("tensor", lambda t, which=which, dc=dc, fc=fc, tb=tb, pq=pq: t.matmul(
                            pq, wslice(which, dc, fc), P.XH[dc][:, tb * TB:(tb + 1) * TB],
                            start=(dc == 0), stop=(dc == DC - 1)),
                            w=wv_ if dc == 0 else (), s=(P.st if dc == DC - 1 else None))
                    mmw = P.st.at()
                    dst = rotQ[:, fc * T + tb * TB: fc * T + (tb + 1) * TB] if which == "q" \
                        else krow[b][:, tb * TB:(tb + 1) * TB]
                    kst_w = [kst_hist[b]] if (which == "k" and tb == 0 and kst_hist[b] is not None) else []
                    if layer == 1:
                        emit_rope(dst, pq, tb, [mmw] + kst_w, P.sv)
                        pa_ev4q[tb] = P.sv.at()
                        done_by = P.sv
                    else:
                        em.op("scalar", lambda sc, dst=dst, pq=pq: sc.copy(dst, pq),
                              w=[mmw] + kst_w, s=P.ss)
                        pa_ev4q[tb] = P.ss.at()
                        done_by = P.ss
                if which == "k":
                    em.op("sync", lambda y, fc=fc, b=b: y.dma_start(
                        out=kin_rows(fc * 128, 128), in_=krow[b][:]),
                        w=[done_by.at()], s=skst[b], k=16)
                    kst_hist[b] = skst[b].at()
            if which == "q":
                qdone = done_by.at()
            proj_done[which] = P.st.at()
        kst_all = [skst[0].at(), skst[1].at()]

        pb_ev = [pa_ev4q[2], pa_ev4q[3]]
        if layer == 1:
            wTv_w = prep_matrix("wv", False, [proj_done["k"]])
        vst_hist = [None, None]
        for ti in range(NT):
            b = ti % 2
            for fb in range(2):
                half = fb
                pv = P.PB[:, half * 512:(half + 1) * 512]
                wv_ = [pb_ev[half]]
                if ti == 0 and fb == 0:
                    wv_.append(vrow_ones)
                    if layer == 1:
                        wv_.append(wTv_w)
                if fb == 0 and vst_hist[b] is not None:
                    wv_.append(vst_hist[b])
                for dc in range(DC):
                    em.op("tensor", lambda t, ti=ti, fb=fb, dc=dc, pv=pv: t.matmul(
                        pv, P.XH[dc][:, ti * 128:(ti + 1) * 128], vslice(dc, fb),
                        start=(dc == 0), stop=(dc == DC - 1)),
                        w=wv_ if dc == 0 else (), s=(P.st if dc == DC - 1 else None))
                mmw = P.st.at()
                em.op("scalar", lambda sc, b=b, fb=fb, pv=pv: sc.copy(
                    vrow[b][:].rearrange("p (h x) -> p h x", x=65)[:, fb * 8:(fb + 1) * 8, 0:64],
                    pv.rearrange("p (h dk) -> p h dk", dk=64)),
                    w=[mmw], s=P.ss)
                pb_ev[half] = P.ss.at()
            em.op("sync", lambda y, ti=ti, b=b: y.dma_start(
                out=vin_rows(ti * 128, 128), in_=vrow[b][:]),
                w=[P.ss.at()], s=svst[b], k=16)
            vst_hist[b] = svst[b].at()
        vst_all = [svst[0].at(), svst[1].at()]
        scope_w.close()
        em.stack = scope

        ccs = em.sem("ccs")
        groups = [[0, 1], [2, 3], [4, 5], [6, 7]]
        em.op("gpsimd", lambda g: g.collective_compute(
            "AllGather", ALU.bypass, replica_groups=groups,
            ins=[kv_in], outs=[kv_out]), w=kst_all + vst_all, s=ccs, k=1)
        cc_done = ccs.at()

        xtb_bf = P.XTBIG[:].bitcast(BF16)
        vx = sbuf(em, "vx", [128, 65 * NH], BF16)

        def vhat(g):
            if g < 31:
                return xtb_bf[:, g * 1040:(g + 1) * 1040]
            return vx[:]
        svh = em.sem("svh")
        stash_w = [getattr(P, "stash_done", None)] if getattr(P, "stash_done", None) else []
        for g in range(KT):
            src_row = (g % 2) * T + (g // 2) * 128
            em.op("sync", lambda y, g=g: y.dma_start(
                out=vhat(g), in_=vout_rows(g % 2, (g // 2) * 128, 128)),
                w=[cc_done] + stash_w if g == 0 else (), s=svh, k=16)
        vhat_w = svh.at()

        def attnT(qb, fc):
            return P.XHBIG[:, qb * (DC * TB) + fc * TB: qb * (DC * TB) + (fc + 1) * TB]
        kh = [sbuf(em, f"kh{i}", [64, S], BF16) for i in range(2)]
        skh = [em.sem("skh0"), em.sem("skh1")]
        kh_used = [None, None]
        qhat = [sbuf(em, f"qhat{i}", [64, TB], BF16) for i in range(2)]
        ptile = [sbuf(em, f"pt{i}", [128, TB], BF16) for i in range(4)]
        rcpb = sbuf(em, "rcpb", [4, TB], BF16)
        rcpf = sbuf(em, "rcpf", [1, TB], F32)
        rbsb = sbuf(em, "rbsb", [64, TB], F32)
        em.op("gpsimd", lambda g: g.memset(rcpb[:], 0.0), s=P.sg)
        rcpb_z = P.sg.at()
        if layer == 1:
            cmr = sbuf(em, "cmr", [128, NT * 256], BF16)
            scm = em.sem("scm")
            cmd = C["cmask_d"]
            em.op("gpsimd", lambda g, cmd=cmd: g.dma_start(out=cmr[:], in_=cmd), s=scm, k=16)
            cm_w = scm.at()

        py_ev = [None, None]
        pt_used = [None] * 4
        qhat_used = [None, None]
        attnT_w = {}
        banks4 = [P.PB[:, 0:512], P.PB[:, 512:1024], P.PA[:, 0:512], P.PA[:, 512:1024]]
        pb_ev4 = [pb_ev[0], pb_ev[1], pa_ev4q[0], pa_ev4q[1]]
        exp_ev = [None] * 4
        for h in range(NH):
            hb = h % 2
            wv_ = [cc_done]
            if kh_used[hb] is not None:
                wv_.append(kh_used[hb])
            for r in range(2):
                em.op("sync", lambda y, r=r, h=h, hb=hb: y.dma_start(
                    out=kh[hb][0:64, r * T:(r + 1) * T],
                    in_=kout_rows(r, h * 64, 64)),
                    w=wv_ if r == 0 else (), s=skh[hb], k=16)
            khw = skh[hb].at()
            qrow = h // 2
            qoff = (h % 2) * 64
            for qb in range(QB):
                qhb = qb % 2
                wq_ = [qhat_used[qhb]] if qhat_used[qhb] is not None else [cc_done]
                em.op("vector", lambda v, qhb=qhb, qrow=qrow, qoff=qoff, qb=qb: v.tensor_copy(
                    qhat[qhb][0:64, :],
                    rotQ[qoff:qoff + 64, qrow * T + qb * TB: qrow * T + (qb + 1) * TB]),
                    w=wq_, s=P.sv)
                qc_w = P.sv.at()
                gmax_t = 2 * (qb * 4 + 3) + 2 if layer == 1 else KT
                pyh = P.PY[:, qhb * 512:(qhb + 1) * 512]

                def emit_passB(g, first, qb=qb, qc_w=qc_w, khw=khw, hb=hb, qhb=qhb):
                    bank = banks4[g % 4]
                    qs = max(0, (g // 2 - qb * 4)) * 128 if layer == 1 else 0
                    n = TB - qs
                    wv_ = [pb_ev4[g % 4]]
                    if first:
                        wv_.append(qc_w)
                    if qb == 0 and g == 0:
                        wv_.append(khw)
                    kc = (g % 2) * T + (g // 2) * 128
                    em.op("tensor", lambda t, bank=bank, qs=qs, n=n, hb=hb, qhb=qhb, kc=kc: t.matmul(
                        bank[:, qs:qs + n], kh[hb][0:64, kc:kc + 128],
                        qhat[qhb][0:64, qs:qs + n], start=True, stop=True),
                        w=wv_, s=P.st)
                    mmw = P.st.at()
                    exw = [mmw]
                    if layer == 1:
                        ti_loc = g // 2 - qb * 4
                        if 0 <= ti_loc < 4:
                            em.op("vector", lambda v, bank=bank, g=g, ti_loc=ti_loc: v.tensor_tensor(
                                bank[:, ti_loc * 128:(ti_loc + 1) * 128],
                                bank[:, ti_loc * 128:(ti_loc + 1) * 128],
                                cmr[:, (g // 2) * 256 + (g % 2) * 128:
                                    (g // 2) * 256 + (g % 2) * 128 + 128],
                                ALU.add), w=[mmw, cm_w], s=P.sv)
                            exw = [P.sv.at()]
                    pslot = g % 4
                    if pt_used[pslot] is not None:
                        exw.append(pt_used[pslot])
                    em.op("scalar", lambda sc, bank=bank, qs=qs, n=n, pslot=pslot: sc.activation(
                        ptile[pslot][:, qs:qs + n], bank[:, qs:qs + n], AF.Exp,
                        scale=scale), w=exw, s=P.ss)
                    pb_ev4[pslot] = P.ss.at()
                    exp_ev[pslot] = P.ss.at()

                def emit_AV(g, qb=qb, pyh=pyh, h=h, qhb=qhb, gmax_t=gmax_t):
                    qs = max(0, (g // 2 - qb * 4)) * 128 if layer == 1 else 0
                    n = TB - qs
                    pslot = g % 4
                    wv2 = [exp_ev[pslot]]
                    if g == 0:
                        wv2.append(vhat_w)
                        if py_ev[qhb] is not None:
                            wv2.append(py_ev[qhb])
                    em.op("tensor", lambda t, g=g, pyh=pyh, qs=qs, n=n, h=h, pslot=pslot, gmax_t=gmax_t: t.matmul(
                        pyh[0:65, qs:qs + n], vhat(g)[:, h * 65:(h + 1) * 65],
                        ptile[pslot][:, qs:qs + n],
                        start=(g == 0), stop=(g == gmax_t - 1)), w=wv2, s=P.st)
                    pt_used[pslot] = P.st.at()

                npro = min(4, gmax_t)
                for g in range(npro):
                    emit_passB(g, g == 0)
                for g in range(gmax_t):
                    emit_AV(g)
                    if g + 4 < gmax_t:
                        emit_passB(g + 4, False)
                av_w = P.st.at()
                qhat_used[qhb] = av_w
                em.op("vector", lambda v, pyh=pyh: v.reciprocal(rcpf[:], pyh[64:65, :]),
                      w=[av_w, rcpb_z] if (h == 0 and qb == 0) else [av_w], s=P.sv)
                rc1 = P.sv.at()
                vfence(em, P, rcpf[:])
                em.op("vector", lambda v: v.tensor_copy(rcpb[0:1, :], rcpf[:]),
                      w=[rc1], s=P.sv)
                rw_ = P.sv.at()
                em.op("tensor", lambda t: t.matmul(
                    P.PM[0:64, 0:512], bc4[0:4, 0:64], rcpb[0:4, :], start=True, stop=True),
                    w=[rw_, (P.ss, P.ss.n)], s=P.st)
                tw = P.st.at()
                em.op("scalar", lambda sc: sc.copy(rbsb[:], P.PM[0:64, 0:512]), w=[tw], s=P.ss)
                rbw = P.ss.at()
                em.op("vector", lambda v, pyh=pyh, h=h, qb=qb: v.tensor_tensor(
                    attnT(qb, h // 2)[(h % 2) * 64:(h % 2) * 64 + 64, :],
                    pyh[0:64, :], rbsb[:], ALU.mult), w=[rbw], s=P.sv)
                py_ev[qhb] = P.sv.at()
                attnT_w[(qb, h)] = P.sv.at()
            kh_used[hb] = P.st.at()
        scores_done = P.sv.at()

        if layer == 0:
            wout_sb = sbuf(em, "wout_sb", [128, DC * 1024], BF16)
            swo = em.sem("swo")
            wo_d = W["W_out"]
            em.op("gpsimd", lambda g, wo_d=wo_d: g.dma_start(
                out=wout_sb[:].rearrange("k (c f) -> k c f", c=DC),
                in_=wo_d.rearrange("(c k) f -> k c f", c=DC)), s=swo, k=16)
            wo_w = swo.at()

            def woslice(fc, cc):
                return wout_sb[:, fc * 1024 + cc * 128: fc * 1024 + (cc + 1) * 128]
        else:
            woT = sbuf(em, "woT", [128, 64 * 128], BF16)
            swo = em.sem("swo")
            wo_d = W["wo"]
            wo_used = [None, None]
            tdone2 = [None, None]
            for fc in range(DC):
                b = fc % 2
                wld = [wo_used[b]] if wo_used[b] is not None else [(P.ss, P.ss.n)]
                em.op("gpsimd", lambda g, fc=fc, b=b, wo_d=wo_d: g.dma_start(
                    out=raw[b][:, 0:1024], in_=wo_d[fc * 128:(fc + 1) * 128, :]), w=wld, s=swo, k=16)
                dw = swo.at()
                for dc in range(DC):
                    pm = P.PM[:, 128:256] if dc % 2 == 0 else P.P8[:, 128:256]
                    wv_ = [dw] if dc == 0 else []
                    if tdone2[dc % 2] is not None:
                        wv_.append(tdone2[dc % 2])
                    wv_.append((P.ss, P.ss.n))
                    em.op("tensor", lambda t, b=b, dc=dc, pm=pm: t.matmul(
                        pm, raw[b][:, dc * 128:(dc + 1) * 128], idn_bf[:],
                        start=True, stop=True), w=wv_, s=P.st)
                    tw = P.st.at()
                    em.op("scalar", lambda sc, dc=dc, fc=fc, pm=pm: sc.copy(
                        woT[:, (dc * DC + fc) * 128:(dc * DC + fc) * 128 + 128], pm),
                        w=[tw], s=P.ss)
                    tdone2[dc % 2] = P.ss.at()
                wo_used[b] = P.ss.at()
            wo_w = P.ss.at()

            def woslice(fc, cc):
                return woT[:, (fc * DC + cc) * 128:(fc * DC + cc) * 128 + 128]

        xrow = [sbuf(em, f"xrow{i}", [128, TB], F32) for i in range(2)]
        sxl = [em.sem("sxl0"), em.sem("sxl1")]
        xrow_used = [None, None]
        pp_ev = [None, None]
        for qb in range(QB):
            for cc in range(DC):
                b = (qb * DC + cc) % 2
                half = b
                pp = P.PM[:, 0:512] if half == 0 else P.P8[:, 0:512]
                wv_ = [xrow_used[b]] if xrow_used[b] is not None else list(stash_w)
                em.op("sync", lambda y, qb=qb, cc=cc, b=b: y.dma_start(
                    out=xrow[b][:], in_=xsrc[cc * 128:(cc + 1) * 128, qb * TB:(qb + 1) * TB]),
                    w=wv_, s=sxl[b], k=16)
                xw = sxl[b].at()
                wv2 = [attnT_w[(qb, NH - 1)], wo_w, scores_done]
                if pp_ev[half] is not None:
                    wv2.append(pp_ev[half])
                for fc in range(DC):
                    em.op("tensor", lambda t, fc=fc, cc=cc, qb=qb, pp=pp: t.matmul(
                        pp, woslice(fc, cc), attnT(qb, fc),
                        start=(fc == 0), stop=(fc == DC - 1)),
                        w=wv2 if fc == 0 else (), s=(P.st if fc == DC - 1 else None))
                mmw = P.st.at()
                em.op("vector", lambda v, cc=cc, qb=qb, b=b, pp=pp: v.tensor_tensor(
                    P.XT[cc][:, qb * TB:(qb + 1) * TB], xrow[b][:], pp, ALU.add),
                    w=[mmw, xw], s=P.sv)
                pp_ev[half] = P.sv.at()
                xrow_used[b] = P.sv.at()
        P.dbg_refs = {"rotQ": rotQ, "kh0": kh[0], "kh1": kh[1]}
        if os.environ.get("KDBG", "") == "1":
            sdbg = em.sem("sdbg")
            wd = [(P.sv, P.sv.n), (P.ss, P.ss.n), (P.st, P.st.n)]
            nc2 = em.nc
            d_attnT = nc2.dram_tensor("dbg_attnT", [128, DC * T], BF16, kind="ExternalOutput").ap()
            d_rotq = nc2.dram_tensor("dbg_rotq", [128, DC * T], BF16, kind="ExternalOutput").ap()
            d_kh0 = nc2.dram_tensor("dbg_kh0", [64, S], BF16, kind="ExternalOutput").ap()
            d_kcoll = nc2.dram_tensor("dbg_kcoll", [2 * D, T], BF16, kind="ExternalOutput").ap()
            d_vcoll = nc2.dram_tensor("dbg_vcoll", [2 * T, NH * 65], BF16, kind="ExternalOutput").ap()
            em.op("sync", lambda y: y.dma_start(out=d_attnT, in_=P.XHBIG[:]), w=wd, s=sdbg, k=16)
            em.op("sync", lambda y: y.dma_start(out=d_rotq, in_=rotQ[:]), s=sdbg, k=16)
            em.op("sync", lambda y: y.dma_start(out=d_kh0, in_=kh[0][:]), s=sdbg, k=16)
            em.op("sync", lambda y: y.dma_start(out=d_kcoll, in_=kv_out[0:2 * KN].rearrange("(p t) -> p t", t=T)), s=sdbg, k=16)
            em.op("sync", lambda y: y.dma_start(out=d_vcoll, in_=kv_out[0:VN].rearrange("(p x) -> p x", x=NH * 65)), s=sdbg, k=16)
            d_vcin = nc2.dram_tensor("dbg_vcin", [T, NH * 65], BF16, kind="ExternalOutput").ap()
            em.op("sync", lambda y: y.dma_start(out=d_vcin, in_=vin_rows(0, T)), s=sdbg, k=16)
            em.barrier_wait("sync", [sdbg.at()])
        em.stack = outer
    P.attn_done = P.sv.at()


def const_inputs(nc):
    C = {}
    C["idn"] = nc.dram_tensor("c_idn", [128, 128], F32, kind="ExternalInput").ap()
    C["bc4_d"] = nc.dram_tensor("c_bc4", [4, 128], F32, kind="ExternalInput").ap()
    C["esel0_d"] = nc.dram_tensor("c_esel0", [E0, E0 * 128], F32, kind="ExternalInput").ap()
    C["esel1_d"] = nc.dram_tensor("c_esel1", [E1, E1 * 128], F32, kind="ExternalInput").ap()
    return C


def load_consts(em, P, C, need=("idn",)):
    scst = em.sem("scst")
    scsg = em.sem("scsg")
    ones_bc = sbuf(em, "c_ones_bc", [1, 128], BF16)
    ones_sum = sbuf(em, "c_ones_sum", [128, 1], BF16)
    em.op("gpsimd", lambda g: g.memset(ones_bc[:], 1.0), s=P.sg)
    em.op("gpsimd", lambda g: g.memset(ones_sum[:], 1.0), s=P.sg)
    bc4 = sbuf(em, "c_bc4_sb", [4, 128], BF16)
    bc4_d = C["bc4_d"]
    em.op("gpsimd", lambda g, bc4_d=bc4_d: g.dma_start(out=bc4[:], in_=bc4_d), s=scsg, k=16)
    C["bc4"] = bc4
    eps = sbuf(em, "c_eps", [1, 1], F32)
    em.op("gpsimd", lambda g: g.memset(eps[:], 1e-5), s=P.sg)
    C["ones_bc"] = ones_bc
    C["ones_sum"] = ones_sum
    C["eps"] = eps
    if "idn" in need:
        idn = sbuf(em, "c_idn_sb", [128, 128], F32)
        idn_d = C["idn"]
        em.op("sync", lambda y, idn_d=idn_d: y.dma_start(out=idn[:], in_=idn_d), s=scst, k=16)
        idnb = sbuf(em, "c_idnb_sb", [128, 128], BF16)
        em.op("gpsimd", lambda g, idn_d=idn_d: g.dma_start(out=idnb[:], in_=idn_d), s=scsg, k=16)
        C["idn"] = idn
        C["idn_bf"] = idnb
    em.barrier_wait("vector", [scst.at(), scsg.at(), (P.sg, P.sg.n)])
    em.barrier_wait("scalar", [scst.at(), scsg.at(), (P.sg, P.sg.n)])
    em.barrier_wait("tensor", [scst.at(), scsg.at(), (P.sg, P.sg.n)])


def build_phase(phase):
    nc = bass.Bass("TRN2", target_bir_lowering=False)
    stack = contextlib.ExitStack()
    em = Em(nc, stack)
    with stack:
        C = const_inputs(nc)
        if phase == "p_in":
            x_tok = nc.dram_tensor("x_tok", [T, D], F32, kind="ExternalInput").ap()
            xt_out = nc.dram_tensor("xt_out", [D, T], F32, kind="ExternalOutput").ap()
            P = make_ctx(em)
            load_consts(em, P, C, need=("idn",))
            emit_tin(em, P, x_tok, C)
            emit_store_xt(em, P, xt_out)
        elif phase == "p_out":
            xt_in = nc.dram_tensor("xt_in", [D, T], F32, kind="ExternalInput").ap()
            x_out = nc.dram_tensor("x_out", [T, D], F32, kind="ExternalOutput").ap()
            P = make_ctx(em)
            load_consts(em, P, C, need=("idn",))
            emit_load_xt(em, P, xt_in)
            emit_tout(em, P, x_out, C)
        elif phase in ("ffn0", "ffn1"):
            layer = 0 if phase == "ffn0" else 1
            xt_in = nc.dram_tensor("xt_in", [D, T], F32, kind="ExternalInput").ap()
            xt_out = nc.dram_tensor("xt_out", [D, T], F32, kind="ExternalOutput").ap()
            W = {}
            if layer == 0:
                W["gate_w"] = nc.dram_tensor("L0_ffn_gate_w", [E0, D, H0], F32, kind="ExternalInput").ap()
                W["gate_b"] = nc.dram_tensor("L0_ffn_gate_b", [E0, H0], F32, kind="ExternalInput").ap()
                W["up_w"] = nc.dram_tensor("L0_ffn_up_w", [E0, D, H0], F32, kind="ExternalInput").ap()
                W["up_b"] = nc.dram_tensor("L0_ffn_up_b", [E0, H0], F32, kind="ExternalInput").ap()
                W["down_w"] = nc.dram_tensor("L0_ffn_down_w", [E0, H0, D], F32, kind="ExternalInput").ap()
                W["down_b"] = nc.dram_tensor("L0_ffn_down_b", [E0, D], F32, kind="ExternalInput").ap()
                W["router_w"] = nc.dram_tensor("L0_ffn_router_w", [D, E0], F32, kind="ExternalInput").ap()
            else:
                W["w1"] = nc.dram_tensor("L1_ffn_w1", [E1, D, F1], F32, kind="ExternalInput").ap()
                W["b1"] = nc.dram_tensor("L1_ffn_b1", [E1, F1], F32, kind="ExternalInput").ap()
                W["w2"] = nc.dram_tensor("L1_ffn_w2", [E1, F1, D], F32, kind="ExternalInput").ap()
                W["b2"] = nc.dram_tensor("L1_ffn_b2", [E1, D], F32, kind="ExternalInput").ap()
                W["router_w"] = nc.dram_tensor("L1_ffn_router_w", [D, E1], F32, kind="ExternalInput").ap()
            P = make_ctx(em)
            load_consts(em, P, C, need=("idn",))
            emit_load_xt(em, P, xt_in)
            P.dbg_out = xt_out
            emit_ffn(em, P, C, W, layer)
            if not getattr(P, "dbg_done", False):
                emit_store_xt(em, P, xt_out)
        elif phase in ("attn0", "attn1"):
            layer = 0 if phase == "attn0" else 1
            xt_in = nc.dram_tensor("xt_in", [D, T], F32, kind="ExternalInput").ap()
            xt_out = nc.dram_tensor("xt_out", [D, T], F32, kind="ExternalOutput").ap()
            W = {}
            if layer == 0:
                W["W_qkv"] = nc.dram_tensor("L0_attn_W_qkv", [D, 3 * D], F32, kind="ExternalInput").ap()
                W["W_out"] = nc.dram_tensor("L0_attn_W_out", [D, D], F32, kind="ExternalInput").ap()
            else:
                for nm in ("wq", "wk", "wv", "wo"):
                    W[nm] = nc.dram_tensor(f"L1_attn_{nm}", [D, D], F32, kind="ExternalInput").ap()
                C["cos_d"] = nc.dram_tensor("c_cos", [32, T], F32, kind="ExternalInput").ap()
                C["sin_d"] = nc.dram_tensor("c_sin", [32, T], F32, kind="ExternalInput").ap()
                C["cmask_d"] = nc.dram_tensor("c_cmask", [128, NT * 256], F32, kind="ExternalInput").ap()
            P = make_ctx(em)
            load_consts(em, P, C, need=("idn",))
            emit_load_xt(em, P, xt_in)
            emit_attn(em, P, C, W, layer, xt_in)
            emit_store_xt(em, P, xt_out)
        else:
            raise ValueError(phase)
        em.flush(phase)
    return nc


def build_fused():
    nc = bass.Bass("TRN2", target_bir_lowering=False)
    stack = contextlib.ExitStack()
    em = Em(nc, stack)
    with stack:
        C = const_inputs(nc)
        C["cos_d"] = nc.dram_tensor("c_cos", [32, T], F32, kind="ExternalInput").ap()
        C["sin_d"] = nc.dram_tensor("c_sin", [32, T], F32, kind="ExternalInput").ap()
        C["cmask_d"] = nc.dram_tensor("c_cmask", [128, NT * 256], F32, kind="ExternalInput").ap()
        xt_in = nc.dram_tensor("xt_in", [D, T], F32, kind="ExternalInput").ap()
        xt_out = nc.dram_tensor("xt_out", [D, T], F32, kind="ExternalOutput").ap()
        W0f = {
            "gate_w": nc.dram_tensor("L0_ffn_gate_w", [E0, D, H0], F32, kind="ExternalInput").ap(),
            "gate_b": nc.dram_tensor("L0_ffn_gate_b", [E0, H0], F32, kind="ExternalInput").ap(),
            "up_w": nc.dram_tensor("L0_ffn_up_w", [E0, D, H0], F32, kind="ExternalInput").ap(),
            "up_b": nc.dram_tensor("L0_ffn_up_b", [E0, H0], F32, kind="ExternalInput").ap(),
            "down_w": nc.dram_tensor("L0_ffn_down_w", [E0, H0, D], F32, kind="ExternalInput").ap(),
            "down_b": nc.dram_tensor("L0_ffn_down_b", [E0, D], F32, kind="ExternalInput").ap(),
            "router_w": nc.dram_tensor("L0_ffn_router_w", [D, E0], F32, kind="ExternalInput").ap(),
        }
        W1f = {
            "w1": nc.dram_tensor("L1_ffn_w1", [E1, D, F1], F32, kind="ExternalInput").ap(),
            "b1": nc.dram_tensor("L1_ffn_b1", [E1, F1], F32, kind="ExternalInput").ap(),
            "w2": nc.dram_tensor("L1_ffn_w2", [E1, F1, D], F32, kind="ExternalInput").ap(),
            "b2": nc.dram_tensor("L1_ffn_b2", [E1, D], F32, kind="ExternalInput").ap(),
            "router_w": nc.dram_tensor("L1_ffn_router_w", [D, E1], F32, kind="ExternalInput").ap(),
        }
        W0a = {
            "W_qkv": nc.dram_tensor("L0_attn_W_qkv", [D, 3 * D], F32, kind="ExternalInput").ap(),
            "W_out": nc.dram_tensor("L0_attn_W_out", [D, D], F32, kind="ExternalInput").ap(),
        }
        W1a = {nm: nc.dram_tensor(f"L1_attn_{nm}", [D, D], F32, kind="ExternalInput").ap()
               for nm in ("wq", "wk", "wv", "wo")}
        P = make_ctx(em)
        P.dbg_out = None
        load_consts(em, P, C, need=("idn",))
        emit_load_xt(em, P, xt_in)
        P.xstash = nc.dram_tensor("xstash", [D, T], F32).ap()
        phases = [("ffn", 0, W0f), ("attn", 0, W0a), ("ffn", 0, W0f),
                  ("ffn", 1, W1f), ("attn", 1, W1a), ("ffn", 1, W1f)]
        for i, (knd, layer, Wm) in enumerate(phases):
            em.prefix = f"p{i}_"
            if i > 0:
                em.phase_barrier([P.st.at(), P.ss.at(), P.sv.at(), P.sg.at()])
                P.xt_ready = P.sv.at()
            if knd == "ffn":
                emit_ffn(em, P, C, Wm, layer)
            else:
                sstash = em.sem("sstash")
                for c in range(DC):
                    em.op("sync", lambda y, c=c: y.dma_start(
                        out=P.xstash[c * 128:(c + 1) * 128, :], in_=P.XT[c]),
                        s=sstash, k=16)
                P.stash_done = sstash.at()
                emit_attn(em, P, C, Wm, layer, P.xstash)
        em.prefix = ""
        em.phase_barrier([P.st.at(), P.ss.at(), P.sv.at(), P.sg.at()])
        emit_store_xt(em, P, xt_out)
        em.flush("fused")
    return nc


_NC_CACHE = {}


def get_phase(phase):
    if phase not in _NC_CACHE:
        _NC_CACHE[phase] = build_phase(phase)
    return _NC_CACHE[phase]


def host_consts():
    C = {}
    C["c_idn"] = np.eye(128, dtype=np.float32)
    bc4 = np.zeros((4, 128), dtype=np.float32)
    bc4[0, :] = 1.0
    C["c_bc4"] = bc4
    e0 = np.zeros((E0, E0 * 128), dtype=np.float32)
    for e in range(E0):
        e0[e, e * 128:(e + 1) * 128] = 1.0
    C["c_esel0"] = e0
    e1 = np.zeros((E1, E1 * 128), dtype=np.float32)
    for e in range(E1):
        e1[e, e * 128:(e + 1) * 128] = 1.0
    C["c_esel1"] = e1
    return C


def host_rope(par):
    t = np.arange(T)
    pos = (t // 128) * 256 + par * 128 + (t % 128)
    powers = np.arange(0, DKH, 2, dtype=np.float64) / DKH
    freqs = 1.0 / (10000.0 ** powers)
    ang = np.outer(freqs, pos.astype(np.float64))
    return np.cos(ang).astype(np.float32), np.sin(ang).astype(np.float32)


def host_cmask(par):
    m = np.zeros((128, NT * 256), dtype=np.float32)
    kk = np.arange(128)[:, None]
    qq = np.arange(128)[None, :]
    tri = np.where(kk > qq, NEG, 0.0).astype(np.float32)
    for ti in range(NT):
        if par == 0:
            m[:, ti * 256:ti * 256 + 128] = tri
            m[:, ti * 256 + 128:ti * 256 + 256] = NEG
        else:
            m[:, ti * 256:ti * 256 + 128] = 0.0
            m[:, ti * 256 + 128:ti * 256 + 256] = tri
    return m


def shard_rows(x):
    out = []
    for c in range(8):
        b, p = c // 2, c % 2
        xb = np.ascontiguousarray(x[b].reshape(32, 128, D)[p::2].reshape(T, D))
        out.append(xb)
    return out


def unshard_rows(parts):
    x = np.zeros((4, S, D), dtype=np.float32)
    for c in range(8):
        b, p = c // 2, c % 2
        xr = x[b].reshape(32, 128, D)
        xr[p::2] = np.asarray(parts[c], dtype=np.float32).reshape(NT, 128, D)
    return x


def run_phase(phase, per_core_inputs):
    nc = get_phase(phase)
    res = run_bass_kernel_spmd(nc, per_core_inputs, core_ids=list(range(8)))
    return res.results


WEIGHT_NAMES = (
    "L0_attn_W_qkv", "L0_attn_W_out",
    "L0_ffn_gate_w", "L0_ffn_gate_b", "L0_ffn_up_w", "L0_ffn_up_b",
    "L0_ffn_down_w", "L0_ffn_down_b", "L0_ffn_router_w",
    "L1_attn_wq", "L1_attn_wk", "L1_attn_wv", "L1_attn_wo",
    "L1_ffn_router_w", "L1_ffn_w1", "L1_ffn_b1", "L1_ffn_w2", "L1_ffn_b2")


def kernel(**inputs):
    if os.environ.get("KRUN", "fused") == "fused":
        return kernel_fused(**inputs)
    return kernel_phases(**inputs)


def kernel_fused(**inputs):
    hc = host_consts()
    xs = shard_rows(np.asarray(inputs["L0_attn_X"], dtype=np.float32))
    xts = [np.ascontiguousarray(xs[c].T) for c in range(8)]
    wts = {k: np.asarray(inputs[k], dtype=np.float32) for k in WEIGHT_NAMES}
    if "fused" not in _NC_CACHE:
        _NC_CACHE["fused"] = build_fused()
    nc = _NC_CACHE["fused"]
    pcs = []
    for c in range(8):
        par = c % 2
        cosr, sinr = host_rope(par)
        d = dict(hc, **wts)
        d["c_cos"] = cosr
        d["c_sin"] = sinr
        d["c_cmask"] = host_cmask(par)
        d["xt_in"] = xts[c]
        pcs.append(d)
    res = run_bass_kernel_spmd(nc, pcs, core_ids=list(range(8)))
    parts = [np.asarray(res.results[c]["xt_out"], dtype=np.float32).T for c in range(8)]
    return unshard_rows(parts)


def kernel_phases(**inputs):
    hc = host_consts()
    xs = shard_rows(np.asarray(inputs["L0_attn_X"], dtype=np.float32))
    xts = [np.ascontiguousarray(xs[c].T) for c in range(8)]

    pc = []
    for c in range(8):
        par = c % 2
        cosr, sinr = host_rope(par)
        pc.append(dict(hc, c_cos=cosr, c_sin=sinr, c_cmask=host_cmask(par)))

    w_ffn0 = {k: np.asarray(inputs[k], dtype=np.float32) for k in (
        "L0_ffn_gate_w", "L0_ffn_gate_b", "L0_ffn_up_w", "L0_ffn_up_b",
        "L0_ffn_down_w", "L0_ffn_down_b", "L0_ffn_router_w")}
    w_ffn1 = {k: np.asarray(inputs[k], dtype=np.float32) for k in (
        "L1_ffn_w1", "L1_ffn_b1", "L1_ffn_w2", "L1_ffn_b2", "L1_ffn_router_w")}
    w_attn0 = {k: np.asarray(inputs[k], dtype=np.float32) for k in (
        "L0_attn_W_qkv", "L0_attn_W_out")}
    w_attn1 = {k: np.asarray(inputs[k], dtype=np.float32) for k in (
        "L1_attn_wq", "L1_attn_wk", "L1_attn_wv", "L1_attn_wo")}

    def strip(d, names):
        return {k: d[k] for k in names}

    seq = [
        ("ffn0", w_ffn0), ("attn0", w_attn0), ("ffn0", w_ffn0),
        ("ffn1", w_ffn1), ("attn1", w_attn1), ("ffn1", w_ffn1),
    ]
    cur = xts
    for phase, wmap in seq:
        need = {"ffn0": ("c_idn", "c_bc4", "c_esel0", "c_esel1"),
                "ffn1": ("c_idn", "c_bc4", "c_esel0", "c_esel1"),
                "attn0": ("c_idn", "c_bc4", "c_esel0", "c_esel1"),
                "attn1": ("c_idn", "c_bc4", "c_esel0", "c_esel1",
                          "c_cos", "c_sin", "c_cmask")}[phase]
        res = run_phase(phase, [dict(strip(pc[c], need), xt_in=cur[c], **wmap) for c in range(8)])
        cur = [r["xt_out"] for r in res]
    parts = [np.asarray(cur[c], dtype=np.float32).T for c in range(8)]
    return unshard_rows(parts)


if __name__ == "__main__":
    pass
